# version 30
# baseline (speedup 1.0000x reference)
import os
from contextlib import ExitStack

import numpy as np
import ml_dtypes

import concourse.bass as bass
import concourse.mybir as mybir
from concourse.bass_utils import run_bass_kernel_spmd

F32 = mybir.dt.float32
BF16 = mybir.dt.bfloat16
I32 = mybir.dt.int32
AF = mybir.ActivationFunctionType
ALU = mybir.AluOpType
AP = bass.AP

S, D, H, DH, FF = 4096, 1024, 16, 64, 4096
C = 512
HC = 256
NCH = S // C
KT = D // 128
NP = H // 2
NT130 = (S // 128) * 130
PH = int(os.environ.get("MK_PHASES", "99"))

EPS_C, NEGH_C, QSA_C, QSB_C, HALF_C, NEG1_C = 0, 1, 2, 3, 4, 5


class Sem:
    __slots__ = ("h", "n", "name")

    def __init__(self, nc, stack, name):
        self.h = stack.enter_context(nc.semaphore(name))
        self.n = 0
        self.name = name


class Kb:
    ENGINES = ("T", "V", "A", "P", "S")
    ENGMAP = {"T": "tensor", "V": "vector", "A": "scalar", "P": "gpsimd", "S": "sync"}

    def __init__(self, nc, stack):
        self.nc = nc
        self.stack = stack
        self.ops = []
        self.watermark = {e: {} for e in self.ENGINES}
        self.queues = []

    def sem(self, name):
        return Sem(self.nc, self.stack, name)

    def op(self, eng, fn, waits=(), inc=None, n=1):
        tok = None
        w = [x for x in waits if x is not None]
        if inc is not None:
            inc.n += n
            tok = (inc, inc.n)
        self.ops.append((eng, w, fn, (inc, n) if inc is not None else None))
        return tok

    def wait(self, eng, toks):
        self.ops.append((eng, [t for t in toks if t is not None], None, None))

    def emit_block(self, name):
        for q in self.queues:
            if q.dirty:
                self.wait(q.eng_name, [(s, s.n) for s in q.sems if s.n > 0])
                q.dirty = False
        ops = self.ops
        self.ops = []
        used = {e for (e, _, _, _) in ops}
        with self.nc.Block(name) as block:
            for ename in self.ENGINES:
                if ename not in used:
                    continue

                def runner(eng, ename=ename, ops=ops):
                    wm = self.watermark[ename]
                    for (e, waits, fn, inc) in ops:
                        if e != ename:
                            continue
                        for (sem, val) in waits:
                            if wm.get(id(sem), -1) >= val:
                                continue
                            eng.wait_ge(sem.h, val)
                            wm[id(sem)] = val
                        if fn is None:
                            continue
                        ins = fn(eng)
                        if inc is not None:
                            sem, n = inc
                            ins.then_inc(sem.h, n)
                getattr(block, self.ENGMAP[ename])(runner)


class DmaQ:
    def __init__(self, kb, eng_name, name, depth=8):
        self.kb = kb
        self.eng_name = eng_name
        self.sems = [kb.sem(f"{name}{i}") for i in range(depth)]
        self.i = 0
        self.dirty = False
        kb.queues.append(self)

    def dma(self, dst, src, waits=()):
        s = self.sems[self.i]
        self.i = (self.i + 1) % len(self.sems)
        prewait = (s, s.n) if s.n > 0 else None
        self.dirty = True
        return self.kb.op(
            self.eng_name,
            lambda eng, d=dst, so=src: eng.dma_start(d, so),
            waits=[*waits, prewait], inc=s, n=16)


def build_program():
    nc = bass.Bass("TRN2", target_bir_lowering=False)

    din = lambda name, shape, dt: nc.dram_tensor(name, shape, dt, kind="ExternalInput")
    xT = din("xT", [D, S], F32)
    wqaT = din("wqaT", [D, D], F32)
    wkaT = din("wkaT", [D, D], F32)
    wvaT = din("wvaT", [D, D], F32)
    woaT = din("woaT", [D, D], F32)
    wqb_d = din("wqb", [D, D], F32)
    wkb_d = din("wkb", [D, D], F32)
    bqb_d = din("bqb", [128, NP], F32)
    bkb_d = din("bkb", [128, NP], F32)
    wvdT_d = din("wvdT", [16, 16 * D], F32)
    wvu_d = din("wvu", [16, H * DH], F32)
    bvdT_d = din("bvdT", [16, 16], F32)
    bvu_d = din("bvu_row", [1, H * DH], F32)
    fc1q = din("fc1q", [D, FF], I32)
    s1rows = din("s1rows", [1, 8 * FF], F32)
    fc1b_d = din("fc1b", [128, 32], F32)
    fc2q = din("fc2q", [FF, D], I32)
    s2rows = din("s2rows", [1, 32 * D], F32)
    fc2b_d = din("fc2b", [128, 8], F32)
    gam_d = din("gam", [128, 8], F32)
    bet_d = din("bet", [128, 8], F32)
    maska_d = din("maska", [128, 4 * 2 * C], BF16)
    maskb_d = din("maskb", [128, 4 * 2 * C], BF16)
    ones_col_d = din("ones_col", [128, 1], BF16)
    ones_row_d = din("ones_row", [1, 128], BF16)
    ematD_d = din("ematD", [33, 128], BF16)

    outT = nc.dram_tensor("outT", [D, S], F32, kind="ExternalOutput")

    xres = nc.dram_tensor("xres", [D, S], F32, kind="Internal")
    qscr = nc.dram_tensor("qscr", [128, NP * S], BF16, kind="Internal")
    kscr = nc.dram_tensor("kscr", [128, NP * S], BF16, kind="Internal")
    vscr = nc.dram_tensor("vscr", [128, NP * NT130], BF16, kind="Internal")
    w1scr = nc.dram_tensor("w1scr", [128, KT * FF], BF16, kind="Internal")
    w2scr = nc.dram_tensor("w2scr", [128, 32 * D], BF16, kind="Internal")

    stack = ExitStack()
    stack.enter_context(nc.allow_low_precision("bf16 matmuls per problem tolerance spec"))
    kb = Kb(nc, stack)
    sT = kb.sem("sT")
    sV = kb.sem("sV")
    sA = kb.sem("sA")
    sP = kb.sem("sP")
    QS = DmaQ(kb, "S", "qs", 8)
    QP = DmaQ(kb, "P", "qp", 8)
    QO = DmaQ(kb, "S", "qo", 4)

    sb = lambda name, shape, dt: stack.enter_context(nc.sbuf_tensor(name, shape, dt))

    consts = sb("consts", [128, 8], F32)
    ones_col = sb("g_ones_col", [128, 1], BF16)
    ones_row = sb("g_ones_row", [1, 128], BF16)
    ematD = sb("g_ematD", [33, 128], BF16)
    bqbs = sb("g_bqbs", [128, NP], F32)
    bkb = sb("g_bkb", [128, NP], F32)
    fc1b = sb("g_fc1b", [128, 32], F32)
    fc2bh = sb("g_fc2bh", [128, 8], F32)
    gam = sb("g_gam", [128, 8], F32)
    bet = sb("g_bet", [128, 8], F32)
    r_mean = sb("g_rmean", [1, C], F32)
    r_var = sb("g_rvar", [1, C], F32)
    r_rstd = r_var
    r_mrs = r_mean
    r_rstd_bf = sb("g_rrstdbf", [1, C], BF16)
    r_mrs_bf = sb("g_rmrsbf", [1, C], BF16)

    def emit_p0():
        for col, val in ((EPS_C, 1e-5), (NEGH_C, -0.5), (QSA_C, 0.125),
                         (QSB_C, 1.0 / 32.0), (HALF_C, 0.5),
                         (NEG1_C, -1.0)):
            kb.op("P", lambda eng, c=col, v=val: eng.memset(consts[:, c:c + 1], v), inc=sP)
        t_bq = QS.dma(bqbs[:, :], bqb_d.ap())
        QS.dma(bkb[:, :], bkb_d.ap())
        t_b2 = QS.dma(fc2bh[:, :], fc2b_d.ap())
        QS.dma(fc1b[:, :], fc1b_d.ap())
        QS.dma(gam[:, :], gam_d.ap())
        QS.dma(bet[:, :], bet_d.ap())
        QP.dma(ones_col[:, :], ones_col_d.ap())
        QP.dma(ones_row[:, :], ones_row_d.ap())
        QP.dma(ematD[:, :], ematD_d.ap())
        kb.op("V", lambda eng: eng.tensor_scalar_mul(bqbs[:, :], bqbs[:, :], 1.0 / 32.0),
              waits=[t_bq], inc=sV)
        kb.op("V", lambda eng: eng.tensor_scalar_mul(fc2bh[:, :], fc2bh[:, :], 0.5),
              waits=[t_b2], inc=sV)
        kb.emit_block("p0")

    def ln_chunk(ctx, src, src_is_f32, x_tok, war_tok, out_bf, affine=None,
                 out_aff=None):
        ps_sx, ps_sq, ps_r, ps_m, cast_buf, sq_buf, t1_buf, st = ctx
        t1 = t2 = None
        buf_tok = {}
        for kt in range(KT):
            cs = kt * C
            par = kt % 2
            if src_is_f32:
                ct = kb.op("V", lambda eng, o=cs, p=par:
                           eng.tensor_copy(cast_buf[:, p * C:(p + 1) * C], src[:, o:o + C]),
                           waits=[x_tok, war_tok, buf_tok.get(("c", par))], inc=sV)
                rhs = lambda p=par: cast_buf[:, p * C:(p + 1) * C]
            else:
                ct = x_tok
                rhs = lambda o=cs: src[:, o:o + C]
            sqt = kb.op("V", lambda eng, p=par, r=rhs:
                        eng.tensor_mul(sq_buf[:, p * C:(p + 1) * C], r(), r()),
                        waits=[ct, war_tok, buf_tok.get(("s", par))], inc=sV)
            t1 = kb.op("T", lambda eng, k=kt, r=rhs:
                       eng.matmul(ps_sx[0:1, :], ones_col[:, :], r(),
                                  start=(k == 0), stop=(k == KT - 1)),
                       waits=[ct] + ([st["free_sx"]] if kt == 0 else []), inc=sT)
            t2 = kb.op("T", lambda eng, k=kt, p=par:
                       eng.matmul(ps_sq[0:1, :], ones_col[:, :],
                                  sq_buf[:, p * C:(p + 1) * C],
                                  start=(k == 0), stop=(k == KT - 1)),
                       waits=[sqt] + ([st["free_sq"]] if kt == 0 else []), inc=sT)
            buf_tok[("c", par)] = t1
            buf_tok[("s", par)] = t2
        m1 = kb.op("V", lambda eng: eng.tensor_scalar_mul(r_mean[:, :], ps_sx[0:1, :],
                                                          1.0 / D),
                   waits=[t1], inc=sV)
        m2 = kb.op("V", lambda eng: eng.tensor_scalar_mul(r_msq[:, :], ps_sq[0:1, :],
                                                          1.0 / D),
                   waits=[t2], inc=sV)
        st["free_sx"], st["free_sq"] = m1, m2
        kb.op("V", lambda eng: eng.tensor_mul(r_var[:, :], r_mean[:, :], r_mean[:, :]),
              inc=sV)
        vt = kb.op("V", lambda eng: eng.tensor_sub(r_var[:, :], r_msq[:, :], r_var[:, :]),
                   inc=sV)
        kb.op("A", lambda eng: eng.activation(r_rstd[:, :], r_var[:, :], AF.Ln,
                                              bias=consts[0:1, EPS_C:EPS_C + 1],
                                              scale=1.0),
              waits=[vt], inc=sA)
        at = kb.op("A", lambda eng: eng.activation(r_rstd[:, :], r_rstd[:, :], AF.Exp,
                                                   bias=0.0,
                                                   scale=consts[0:1, NEGH_C:NEGH_C + 1]),
                   inc=sA)
        kb.op("V", lambda eng: eng.tensor_mul(r_mrs[:, :], r_mean[:, :], r_rstd[:, :]),
              waits=[at], inc=sV)
        kb.op("V", lambda eng: eng.tensor_copy(r_rstd_bf[:, :], r_rstd[:, :]),
              waits=[at], inc=sV)
        rt = kb.op("V", lambda eng: eng.tensor_copy(r_mrs_bf[:, :], r_mrs[:, :]), inc=sV)
        b1 = kb.op("T", lambda eng: eng.matmul(ps_r[:, :], ones_row[:, :], r_rstd_bf[:, :],
                                               start=True, stop=True),
                   waits=[rt, st["free_r"]], inc=sT)
        b2 = kb.op("T", lambda eng: eng.matmul(ps_m[:, :], ones_row[:, :], r_mrs_bf[:, :],
                                               start=True, stop=True),
                   waits=[st["free_m"]], inc=sT)
        out_tok = None
        for kt in range(KT):
            cs = kt * C
            par = kt % 2
            kb.op("V", lambda eng, o=cs, p=par:
                  eng.tensor_tensor(t1_buf[:, p * C:(p + 1) * C], src[:, o:o + C],
                                    ps_r[:, :], ALU.mult),
                  waits=[b1, x_tok, war_tok], inc=sV)
            if affine is None:
                out_tok = kb.op("V", lambda eng, o=cs, p=par:
                                eng.tensor_tensor(out_bf[:, o:o + C],
                                                  t1_buf[:, p * C:(p + 1) * C],
                                                  ps_m[:, :], ALU.subtract),
                                waits=[b2], inc=sV)
            else:
                g_sb, b_sb = affine
                nrm = kb.op("V", lambda eng, p=par:
                            eng.tensor_tensor(out_bf[:, p * C:(p + 1) * C],
                                              t1_buf[:, p * C:(p + 1) * C],
                                              ps_m[:, :], ALU.subtract),
                            waits=[b2, out_tok], inc=sV)
                out_tok = kb.op("A", lambda eng, o=cs, k=kt, p=par:
                                eng.activation(out_aff[:, o:o + C],
                                               out_bf[:, p * C:(p + 1) * C], AF.Identity,
                                               bias=b_sb[:, k:k + 1],
                                               scale=g_sb[:, k:k + 1]),
                                waits=[nrm], inc=sA)
        st["free_r"], st["free_m"] = out_tok, out_tok
        return out_tok

    def wload(dst_sb, src_dram, q):
        dst = AP(dst_sb, 0, [[KT * D, 128], [D, KT], [1, D]])
        src = AP(src_dram, 0, [[D, 128], [128 * D, KT], [1, D]])
        return q.dma(dst, src)

    def emit_prep(variant):
        seg = ExitStack()
        pfx = f"pr{variant}"
        s_ = lambda name, shape, dt: seg.enter_context(
            nc.sbuf_tensor(f"{pfx}_{name}", shape, dt))
        ps_ = lambda name, shape: seg.enter_context(
            nc.psum_tensor(f"{pfx}_ps_{name}", shape, F32))
        wq_l = s_("wq", [128, KT * D], BF16)
        wk_l = s_("wk", [128, KT * D], BF16)
        wv_m = s_("wv", [128, KT * D], BF16)
        x_bf = s_("xbf", [128, KT * C], BF16)
        y_sb = s_("y", [128, 2 * KT * C], BF16)
        qch = s_("qch", [128, KT * C], BF16)
        kch = s_("kch", [128, KT * C], BF16)
        vch = s_("vch", [128, NP * 520], BF16)
        sq_buf = s_("sq", [128, 2 * C], BF16)
        t1_buf = s_("t1", [128, C], BF16)
        rb_r = s_("rbr", [128, C], BF16)
        rb_m = s_("rbm", [128, C], BF16)
        ps_sx = ps_("sx", [1, C])
        ps_sq = ps_("sq", [1, C])
        ps_r = ps_("r", [128, C])
        ps_m = ps_("m", [128, C])
        ps_qk = [ps_("qk0", [128, C]), ps_("qk1", [128, C])]
        ps_v = [ps_("v0", [128, C]), ps_("v1", [128, C])]
        st = {"free_sx": None, "free_sq": None, "free_r": None, "free_m": None,
              "xbf_free": None, "y_used": [None, None], "sum_toks": None,
              "row_tok": None, "xtok": None, "norm_done": [None] * (NCH + 1),
              "rb_used": None}
        corder = (list(range(NCH)) if variant == "a"
                  else list(range(NCH - 1, -1, -1)))
        x_src = xT if variant == "a" else xres

        def x_load(c_, sliced=False):
            if sliced:
                st["xtok"] = [
                    QP.dma(AP(x_bf, k2 * C, [[KT * C, 128], [1, C]]),
                           AP(x_src, k2 * 128 * S + c_ * C, [[S, 128], [1, C]]),
                           waits=[st["xbf_free"]])
                    for k2 in range(KT)]
            else:
                t_ = QP.dma(
                    AP(x_bf, 0, [[KT * C, 128], [C, KT], [1, C]]),
                    AP(x_src, c_ * C, [[S, 128], [128 * S, KT], [1, C]]),
                    waits=[st["xbf_free"]])
                st["xtok"] = [t_] * KT

        x_load(corder[0], sliced=True)

        if variant == "a":
            tw_q = wload(wq_l, wqaT, QP)
            tw_k = wload(wk_l, wkaT, QP)
            tw_v = wload(wv_m, wvaT, QP)
            biasb_sb = None
        else:
            wvdt_sb = s_("wvdt", [16, 16 * D], BF16)
            wvu_sb = s_("wvu", [16, H * DH], BF16)
            bvdT_sb = s_("bvdT", [16, 16], BF16)
            bvu_sb = s_("bvu", [1, H * DH], F32)
            beff = s_("beff", [1, H * DH], F32)
            beff_bf = s_("beffbf", [1, H * DH], BF16)
            biasb_sb = s_("biasb", [128, H * DH], F32)
            t1w = QP.dma(wvdt_sb[:, :], wvdT_d.ap())
            t2w = QP.dma(wvu_sb[:, :], wvu_d.ap())
            t3w = QP.dma(bvdT_sb[:, :], bvdT_d.ap())
            t4w = QS.dma(bvu_sb[:, :], bvu_d.ap())
            tw_q = wload(wq_l, wqb_d, QP)
            tw_k = wload(wk_l, wkb_d, QP)
            ev_toks = []
            for et in range(KT):
                for hp in range(NP):
                    bank = ps_qk[(et * NP + hp) % 2]
                    free = ev_toks[-2] if len(ev_toks) >= 2 else None
                    mt = None
                    for h01 in range(2):
                        h = 2 * hp + h01
                        mt = kb.op("T", lambda eng, e=et, hh=h, o=h01 * 64, b=bank:
                                   eng.matmul(b[:, o:o + 64],
                                              wvdt_sb[0:16,
                                                      hh * D + e * 128:hh * D + (e + 1) * 128],
                                              wvu_sb[0:16, hh * DH:(hh + 1) * DH],
                                              start=True, stop=True),
                                   waits=[t1w, t2w, free] if h01 == 0 else [],
                                   inc=sT if h01 == 1 else None)
                    ev_toks.append(kb.op("V", lambda eng, e=et, p=hp, b=bank:
                                         eng.tensor_copy(
                                             wv_m[:, e * D + p * 128:e * D + (p + 1) * 128],
                                             b[:, 0:128]),
                                         waits=[mt], inc=sV))
            tw_v = ev_toks[-1]
            rowtoks = {}
            for h in range(H):
                bank = ps_v[h // 8]
                rowtoks[h] = kb.op("T", lambda eng, hh=h, b=bank:
                                   eng.matmul(b[0:1, (hh % 8) * 64:(hh % 8 + 1) * 64],
                                              bvdT_sb[0:16, hh:hh + 1],
                                              wvu_sb[0:16, hh * DH:(hh + 1) * DH],
                                              start=True, stop=True),
                                   waits=[t2w, t3w],
                                   inc=sT if h in (7, 15) else None)
            kb.op("V", lambda eng: eng.tensor_tensor(beff[0:1, 0:512], ps_v[0][0:1, :],
                                                     bvu_sb[0:1, 0:512], ALU.add),
                  waits=[rowtoks[7], t4w], inc=sV)
            kb.op("V", lambda eng: eng.tensor_tensor(beff[0:1, 512:1024], ps_v[1][0:1, :],
                                                     bvu_sb[0:1, 512:1024], ALU.add),
                  waits=[rowtoks[15]], inc=sV)
            bt = kb.op("V", lambda eng: eng.tensor_copy(beff_bf[:, :], beff[:, :]), inc=sV)
            h1 = kb.op("T", lambda eng: eng.matmul(ps_r[:, :], ones_row[:, :],
                                                   beff_bf[0:1, 0:512],
                                                   start=True, stop=True),
                       waits=[bt], inc=sT)
            h2 = kb.op("T", lambda eng: eng.matmul(ps_m[:, :], ones_row[:, :],
                                                   beff_bf[0:1, 512:1024],
                                                   start=True, stop=True),
                       inc=sT)
            kb.op("V", lambda eng: eng.tensor_copy(biasb_sb[:, 0:512], ps_r[:, :]),
                  waits=[h1], inc=sV)
            bb2 = kb.op("V", lambda eng: eng.tensor_copy(biasb_sb[:, 512:1024], ps_m[:, :]),
                        waits=[h2], inc=sV)
            st["free_r"] = bb2
            st["free_m"] = bb2

        def ln_stats(c_):
            xtoks = st["xtok"]
            t1 = t2 = None
            buf_tok = {}
            for kt in range(KT):
                cs = kt * C
                par = kt % 2
                sqt = kb.op("V", lambda eng, o=cs, p=par:
                            eng.tensor_mul(sq_buf[:, p * C:(p + 1) * C],
                                           x_bf[:, o:o + C], x_bf[:, o:o + C]),
                            waits=[xtoks[kt], buf_tok.get(par)], inc=sV)
                t1 = kb.op("T", lambda eng, k=kt, o=cs:
                           eng.matmul(ps_sx[0:1, :], ones_col[:, :], x_bf[:, o:o + C],
                                      start=(k == 0), stop=(k == KT - 1)),
                           waits=[xtoks[kt]] + ([st["free_sx"]] if kt == 0 else []),
                           inc=sT)
                t2 = kb.op("T", lambda eng, k=kt, p=par:
                           eng.matmul(ps_sq[0:1, :], ones_col[:, :],
                                      sq_buf[:, p * C:(p + 1) * C],
                                      start=(k == 0), stop=(k == KT - 1)),
                           waits=[sqt] + ([st["free_sq"]] if kt == 0 else []), inc=sT)
                buf_tok[par] = t2
            st["sum_toks"] = (t1, t2)

        def ln_rows():
            t1, t2 = st["sum_toks"]
            m1 = kb.op("V", lambda eng: eng.tensor_scalar_mul(r_mean[:, :],
                                                              ps_sx[0:1, :], 1.0 / D),
                       waits=[t1], inc=sV)
            kb.op("V", lambda eng: eng.tensor_mul(r_var[:, :], r_mean[:, :],
                                                  r_mean[:, :]), inc=sV)
            vt = kb.op("V", lambda eng: eng.scalar_tensor_tensor(
                r_var[:, :], ps_sq[0:1, :], 1.0 / D, r_var[:, :],
                ALU.mult, ALU.subtract), waits=[t2], inc=sV)
            st["free_sx"], st["free_sq"] = m1, vt
            kb.op("A", lambda eng: eng.activation(r_rstd[:, :], r_var[:, :], AF.Ln,
                                                  bias=consts[0:1, EPS_C:EPS_C + 1],
                                                  scale=1.0),
                  waits=[vt], inc=sA)
            at = kb.op("A", lambda eng: eng.activation(r_rstd[:, :], r_rstd[:, :],
                                                       AF.Exp, bias=0.0,
                                                       scale=consts[0:1, NEGH_C:NEGH_C + 1]),
                       inc=sA)
            kb.op("V", lambda eng: eng.tensor_mul(r_mrs[:, :], r_mean[:, :],
                                                  r_rstd[:, :]),
                  waits=[at], inc=sV)
            kb.op("V", lambda eng: eng.tensor_copy(r_rstd_bf[:, :], r_rstd[:, :]),
                  waits=[at], inc=sV)
            st["row_tok"] = kb.op("V", lambda eng: eng.tensor_copy(r_mrs_bf[:, :],
                                                                   r_mrs[:, :]), inc=sV)

        def ln_norm(c_):
            yh = (c_ % 2) * KT * C
            b1 = kb.op("T", lambda eng: eng.matmul(ps_r[:, :], ones_row[:, :],
                                                   r_rstd_bf[:, :],
                                                   start=True, stop=True),
                       waits=[st["row_tok"], st["free_r"]], inc=sT)
            b2 = kb.op("T", lambda eng: eng.matmul(ps_m[:, :], ones_row[:, :],
                                                   r_mrs_bf[:, :],
                                                   start=True, stop=True),
                       waits=[st["free_m"]], inc=sT)
            e1 = kb.op("V", lambda eng: eng.tensor_copy(rb_r[:, :], ps_r[:, :]),
                       waits=[b1, st["rb_used"]], inc=sV)
            e2 = kb.op("V", lambda eng: eng.tensor_copy(rb_m[:, :], ps_m[:, :]),
                       waits=[b2], inc=sV)
            st["free_r"], st["free_m"] = e1, e2
            out_tok = None
            for kt in range(KT):
                cs = kt * C
                kb.op("V", lambda eng, o=cs:
                      eng.tensor_tensor(t1_buf[:, 0:C],
                                        x_bf[:, o:o + C], rb_r[:, :], ALU.mult),
                      waits=[e1, st["y_used"][c_ % 2]], inc=sV)
                out_tok = kb.op("V", lambda eng, o=cs, y0=yh:
                                eng.tensor_tensor(y_sb[:, y0 + o:y0 + o + C],
                                                  t1_buf[:, 0:C],
                                                  rb_m[:, :], ALU.subtract),
                                waits=[e2], inc=sV)
            st["rb_used"] = out_tok
            st["xbf_free"] = out_tok
            st["norm_done"][c_] = out_tok
            return out_tok

        ln_stats(corder[0])
        ln_rows()
        ln_norm(corder[0])

        evac = {"q": [], "k": []}
        tok_vstore = []
        for oi, c_ in enumerate(corder):
            yh = (c_ % 2) * KT * C
            ytok = st["norm_done"][c_]
            if oi + 1 < NCH:
                x_load(corder[oi + 1])
            last_proj = None
            for nm, wsb, wtok, out_sb, sc_col, bias_sb, scr in (
                    ("q", wq_l, tw_q, qch, (QSA_C if variant == "a" else QSB_C),
                     (bqbs if variant == "b" else None), qscr),
                    ("k", wk_l, tw_k, kch, None,
                     (bkb if variant == "b" else None), kscr)):
                evl = evac[nm]
                for jt in range(KT):
                    bank = ps_qk[jt % 2]
                    free = evl[-2] if len(evl) >= 2 else None
                    mt = None
                    for kt in range(KT):
                        mt = kb.op("T", lambda eng, b=bank, w=wsb, j=jt, k=kt, y0=yh:
                                   eng.matmul(b[:, :],
                                              w[:, k * D + j * 128:k * D + (j + 1) * 128],
                                              y_sb[:, y0 + k * C:y0 + (k + 1) * C],
                                              start=(k == 0), stop=(k == KT - 1)),
                                   waits=[ytok, wtok, free] if kt == 0 else [],
                                   inc=sT if kt == KT - 1 else None)
                    sc_ap = None if sc_col is None else consts[:, sc_col:sc_col + 1]

                    def mkfn(b=bank, o=out_sb, j=jt, sc=sc_ap, bs=bias_sb):
                        def fn(eng):
                            return eng.activation(
                                o[:, j * C:(j + 1) * C], b[:, :],
                                AF.Identity if bs is not None else AF.Copy,
                                bias=(bs[:, j:j + 1] if bs is not None else 0.0),
                                scale=(sc if sc is not None else 1.0))
                        return fn
                    evl.append(kb.op("A", mkfn(), waits=[mt], inc=sA))
                    last_proj = mt
                    if nm == "q" and oi + 1 < NCH:
                        if jt == 2:
                            ln_stats(corder[oi + 1])
                        elif jt == 4:
                            ln_rows()
                        elif jt == 7:
                            ln_norm(corder[oi + 1])
                dstq = AP(scr, c_ * C, [[NP * S, 128], [S, KT], [1, C]])
                srcq = AP(out_sb, 0, [[KT * C, 128], [C, KT], [1, C]])
                QS.dma(dstq, srcq, waits=[evl[-1]])

            v_evac = []
            war_v = tok_vstore[-1] if tok_vstore else None
            for ts in range(4):
                for hf in range(2):
                    bank = ps_v[hf]
                    free = v_evac[-2] if len(v_evac) >= 2 else None
                    mt = None
                    for kt in range(KT):
                        mt = kb.op("T", lambda eng, b=bank, t=ts, k=kt, h=hf, y0=yh:
                                   eng.matmul(b[:, :],
                                              y_sb[:, y0 + k * C + t * 128:
                                                   y0 + k * C + (t + 1) * 128],
                                              wv_m[:, k * D + h * 512:k * D + (h + 1) * 512],
                                              start=(k == 0), stop=(k == KT - 1)),
                                   waits=[ytok, tw_v, free] if kt == 0 else [],
                                   inc=sT if kt == KT - 1 else None)
                    off = ts * 130 + hf * 4 * 520
                    dst4 = AP(vch, off, [[NP * 520, 128], [520, 4], [65, 2], [1, 64]])
                    src4 = AP(bank, 0, [[C, 128], [128, 4], [64, 2], [1, 64]])
                    if variant == "a":
                        v_evac.append(kb.op("V", lambda eng, d=dst4, s2=src4:
                                            eng.tensor_copy(d, s2),
                                            waits=[mt, war_v], inc=sV))
                    else:
                        b4 = AP(biasb_sb, hf * 512,
                                [[H * DH, 128], [128, 4], [64, 2], [1, 64]])
                        v_evac.append(kb.op("V", lambda eng, d=dst4, s2=src4, b=b4:
                                            eng.tensor_tensor(d, s2, b, ALU.add),
                                            waits=[mt, war_v], inc=sV))
                    last_proj = mt
            otok = kb.op("P", lambda eng:
                         eng.memset(AP(vch, 64, [[NP * 520, 128], [65, 64]]), 1.0),
                         waits=[war_v], inc=sP)
            dstv = AP(vscr, c_ * 520, [[NP * NT130, 128], [NT130, NP], [1, 520]])
            srcv = AP(vch, 0, [[NP * 520, 128], [520, NP], [1, 520]])
            tok_vstore.append(QS.dma(dstv, srcv, waits=[v_evac[-1], otok]))
            st["y_used"][c_ % 2] = last_proj

        kb.emit_block(f"prep_{variant}")
        seg.close()

    def emit_attend(variant):
        seg = ExitStack()
        pfx = f"at{variant}"
        s_ = lambda name, shape, dt: seg.enter_context(
            nc.sbuf_tensor(f"{pfx}_{name}", shape, dt))
        ps_ = lambda name, shape: seg.enter_context(
            nc.psum_tensor(f"{pfx}_ps_{name}", shape, F32))
        kbuf = [s_(f"k{i}", [128, S], BF16) for i in range(5)]
        vbuf = [s_(f"v{i}", [128, NT130], BF16) for i in range(5)]
        qbuf = s_("q", [128, 2 * NP * C], BF16)
        x_sb = s_("x", [128, 2 * KT * C], F32)
        attn_sb = s_("attn", [128, 2 * NP * C], BF16)
        p_sb = s_("p", [128, 3 * 2 * C], BF16)
        rb_sb = s_("rb", [128, C], BF16)
        stage = s_("stage", [128, 2 * C], F32)
        mask_sb = s_("mask", [128, 4 * 2 * C], BF16)
        lnr = s_("lnr", [33, C], F32)
        rd_sb = s_("rd", [33, C], BF16)
        wo_l = s_("wo", [128, KT * D], BF16) if variant == "a" else None
        ps_s = [ps_("s0", [128, 2 * C]), ps_("s1", [128, 2 * C])]
        ps_pv = [ps_("pv0", [128, C]), ps_("pv1", [128, C])]
        ps_db = [ps_("db0", [128, C]), ps_("db1", [128, C])]

        x_in = xT if variant == "a" else xres
        kb.op("P", lambda eng: eng.memset(rd_sb[:, :], 0.0), inc=sP)
        orow = s_("orow", [1, C], BF16)
        ofill = kb.op("P", lambda eng: eng.memset(orow[:, :], 1.0), inc=sP)
        fill0 = kb.op("T", lambda eng: eng.matmul(ps_db[0][:, :], ones_row[:, :],
                                                  orow[:, :], start=True, stop=True),
                      waits=[ofill], inc=sT)
        fill1 = kb.op("T", lambda eng: eng.matmul(ps_db[1][:, :], ones_row[:, :],
                                                  orow[:, :], start=True, stop=True),
                      inc=sT)
        db_fill = [fill0, fill1]

        dq_slices = []
        if variant == "a":
            dqf = s_("dqf", [128, 2048], BF16)
            dqs = s_("dqs", [128, 2048], BF16)
            dqw = s_("dqw", [128, 2048], BF16)
            dq_state = {"stt": None, "out": None, "in": None}

            def mk_dq(qsrc, ssrc, wscr, kt, col0, width, qw):
                def emit_dma():
                    tq = QP.dma(AP(dqf, 0, [[2048, 128], [1, width]]),
                                AP(qsrc, kt * 128 * qw + col0, [[qw, 128], [1, width]]),
                                waits=[dq_state["stt"]])
                    ts_ = QP.dma(AP(dqs, 0, [[2048, 128], [1, width]]),
                                 AP(ssrc, kt * qw + col0, [[0, 128], [1, width]]),
                                 waits=[dq_state["stt"]])
                    dq_state["in"] = (tq, ts_)

                def emit_compute():
                    w_ = dq_state["out"]
                    tq, ts_ = dq_state["in"]
                    h2 = width // 2
                    kb.op("V", lambda eng, w2=h2:
                          eng.scalar_tensor_tensor(dqw[:, 0:w2], dqf[:, 0:w2],
                                                   128.0, dqs[:, 0:w2],
                                                   ALU.subtract, ALU.mult),
                          waits=[tq, ts_, w_], inc=sV)
                    stt = kb.op("V", lambda eng, o=h2, w2=h2:
                                eng.scalar_tensor_tensor(dqw[:, o:o + w2],
                                                         dqf[:, o:o + w2],
                                                         128.0, dqs[:, o:o + w2],
                                                         ALU.subtract, ALU.mult),
                                inc=sV)
                    dq_state["stt"] = stt
                    dq_state["out"] = QP.dma(
                        AP(wscr, kt * qw + col0, [[KT * FF, 128], [1, width]]),
                        AP(dqw, 0, [[2048, 128], [1, width]]), waits=[stt])
                return emit_dma, emit_compute
            for kt in range(KT):
                for hf2 in range(2):
                    dq_slices.append(mk_dq(fc1q, s1rows, w1scr, kt, hf2 * 2048,
                                           2048, FF))
            for kt in range(32):
                dq_slices.append(mk_dq(fc2q, s2rows, w2scr, kt, 0, 1024, D))

        def kv_range(c_):
            return (0, 4 * c_ + 4) if variant == "a" else (4 * c_, 32)

        tok_kv = {}
        kv_free = {}

        def load_kv(c_, hp, buf, prev_key):
            j0, j1 = kv_range(c_)
            n = j1 - j0
            nh = (n + 1) // 2
            wk_, wv_ = kv_free.get(prev_key, (None, None))
            tk1 = QP.dma(AP(kbuf[buf], 0, [[S, 128], [1, nh * 128]]),
                         AP(kscr, hp * S + j0 * 128, [[NP * S, 128], [1, nh * 128]]),
                         waits=[wk_])
            tv1 = QP.dma(AP(vbuf[buf], 0, [[NT130, 128], [1, nh * 130]]),
                         AP(vscr, hp * NT130 + j0 * 130,
                            [[NP * NT130, 128], [1, nh * 130]]),
                         waits=[wv_])
            if n > nh:
                tk2 = QP.dma(AP(kbuf[buf], nh * 128, [[S, 128], [1, (n - nh) * 128]]),
                             AP(kscr, hp * S + (j0 + nh) * 128,
                                [[NP * S, 128], [1, (n - nh) * 128]]))
                tv2 = QP.dma(AP(vbuf[buf], nh * 130,
                                [[NT130, 128], [1, (n - nh) * 130]]),
                             AP(vscr, hp * NT130 + (j0 + nh) * 130,
                                [[NP * NT130, 128], [1, (n - nh) * 130]]))
            else:
                tk2, tv2 = tk1, tv1
            tok_kv[(c_, hp)] = (tk1, tk2, tv1, tv2, nh, buf)

        exp_free = {0: None, 1: None}
        p_free = {0: None, 1: None, 2: None}
        norm_free = {0: None, 1: None}
        state = {"rd_free": None, "rb_free": None, "db_free": [None, None],
                 "tok_x_free": [None, None], "tok_x": [None, None],
                 "wo_bank_free": None, "wo_norm_toks": [None, None]}
        attn_free = {0: [None] * NP, 1: [None] * NP}
        tok_q = [None, None]
        tok_q_last = {}
        stage_hist = []
        deferred = []
        def_wo = []
        chunk_order = (list(range(NCH)) if variant == "a"
                       else list(range(NCH - 1, -1, -1)))
        cpos = {c2: i2 for i2, c2 in enumerate(chunk_order)}
        pairs = [(c_, hp) for c_ in chunk_order for hp in range(NP)]
        for ii in range(5):
            load_kv(*pairs[ii], ii, None)
        tok_mask = QS.dma(mask_sb[:, :], (maska_d if variant == "a" else maskb_d).ap())
        tok_wo = wload(wo_l, woaT, QP) if variant == "a" else None

        def q_load(c_):
            half = c_ % 2
            oi = cpos[c_]
            return QS.dma(
                AP(qbuf, half * NP * C, [[2 * NP * C, 128], [C, NP], [1, C]]),
                AP(qscr, c_ * C, [[NP * S, 128], [S, NP], [1, C]]),
                waits=[tok_q_last.get(chunk_order[oi - 2]) if oi >= 2 else None])

        tok_q[chunk_order[0] % 2] = q_load(chunk_order[0])

        def emit_qk(c_, hp, i, tk, buf):
            sp = i % 2
            sbank = ps_s[sp]
            qo = (c_ % 2) * NP * C + hp * C
            kb.op("T", lambda eng, b=sbank, kk=kbuf[buf], ii=i, q0=qo:
                  eng.matmul(b[:, 0:C], kk[0:64, ii * 128:(ii + 1) * 128],
                             qbuf[0:64, q0:q0 + C], start=True, stop=True,
                             tile_position=(0, 0)),
                  waits=[tk, tok_q[c_ % 2], exp_free[sp]])
            return kb.op("T", lambda eng, b=sbank, kk=kbuf[buf], ii=i, q0=qo:
                         eng.matmul(b[:, C:2 * C],
                                    kk[64:128, ii * 128:(ii + 1) * 128],
                                    qbuf[64:128, q0:q0 + C],
                                    start=True, stop=True,
                                    tile_position=(64, 0)),
                         inc=sT)

        def emit_tail(c_, hp, par, xh, pvb, last_pvden):
            dbb = ps_db[par]
            kb.op("A", lambda eng, b=dbb:
                  eng.activation(lnr[0:33, :], b[0:33, :], AF.Ln),
                  waits=[last_pvden, state["rd_free"]], inc=sA)
            recip_tok = kb.op("A", lambda eng:
                              eng.activation(rd_sb[0:33, :], lnr[0:33, :], AF.Exp,
                                             bias=0.0,
                                             scale=consts[0:33, NEG1_C:NEG1_C + 1]),
                              inc=sA)
            tb = kb.op("T", lambda eng, b=dbb:
                       eng.matmul(b[:, :], ematD[:, :], rd_sb[:, :],
                                  start=True, stop=True),
                       waits=[recip_tok], inc=sT)
            state["rd_free"] = tb
            rbt = kb.op("V", lambda eng, b=dbb: eng.tensor_copy(rb_sb[:, :], b[:, :]),
                        waits=[tb, state["rb_free"]], inc=sV)
            state["db_free"][par] = rbt
            ah = xh * NP * C
            nt_ = kb.op("V", lambda eng, p=hp, b=pvb, a0=ah:
                        eng.tensor_tensor(attn_sb[:, a0 + p * C:a0 + (p + 1) * C],
                                          b[:, :], rb_sb[:, :], ALU.mult),
                        waits=[rbt, attn_free[xh][hp]], inc=sV)
            state["rb_free"] = nt_
            norm_free[par] = nt_
            if variant == "b":
                war = stage_hist[-2][1] if len(stage_hist) >= 2 else None
                st_ = kb.op("V", lambda eng, p=hp, sp2=par, x0=xh * KT * C, a0=ah:
                            eng.tensor_tensor(stage[:, sp2 * C:(sp2 + 1) * C],
                                              attn_sb[:, a0 + p * C:a0 + (p + 1) * C],
                                              x_sb[:, x0 + p * C:x0 + (p + 1) * C],
                                              ALU.add),
                            waits=[nt_, state["tok_x"][xh][hp], war], inc=sV)
                attn_free[xh][hp] = st_
                dmat = QO.dma(AP(xres, hp * 128 * S + c_ * C, [[S, 128], [1, C]]),
                              stage[:, par * C:(par + 1) * C], waits=[st_])
                stage_hist.append((st_, dmat))
                if hp == NP - 1:
                    state["tok_x_free"][xh] = st_

        def emit_wo_group(c_, it, xh):
            bank = ps_pv[1]
            norm_toks = [state["wo_norm_toks"][0], state["wo_norm_toks"][1],
                         norm_free[1]]
            ah = xh * NP * C
            mt = None
            for jt in range(KT):
                mt = kb.op("T", lambda eng, b=bank, i2=it, j2=jt, a0=ah:
                           eng.matmul(b[:, :],
                                      wo_l[:, j2 * D + i2 * 128:
                                           j2 * D + (i2 + 1) * 128],
                                      attn_sb[:, a0 + j2 * C:a0 + (j2 + 1) * C],
                                      start=(j2 == 0), stop=(j2 == KT - 1)),
                           waits=([tok_wo, state["wo_bank_free"]] + norm_toks)
                           if jt == 0 else [],
                           inc=sT if jt == KT - 1 else None)
            war = stage_hist[-2][1] if len(stage_hist) >= 2 else None
            st_ = kb.op("V", lambda eng, b=bank, i2=it, sp2=it % 2, x0=xh * KT * C:
                        eng.tensor_tensor(stage[:, sp2 * C:(sp2 + 1) * C],
                                          b[:, :],
                                          x_sb[:, x0 + i2 * C:x0 + (i2 + 1) * C],
                                          ALU.add),
                        waits=[mt, state["tok_x"][xh][it], war], inc=sV)
            dmat = QO.dma(AP(xres, it * 128 * S + c_ * C, [[S, 128], [1, C]]),
                          stage[:, (it % 2) * C:(it % 2 + 1) * C], waits=[st_])
            stage_hist.append((st_, dmat))
            state["wo_bank_free"] = st_
            if it == KT - 1:
                for p2 in range(NP):
                    attn_free[xh][p2] = mt
                state["tok_x_free"][xh] = st_

        pending_qk = None
        ucnt = [0]
        for idx, (c_, hp) in enumerate(pairs):
            par = hp % 2
            xh = c_ % 2
            j0, j1 = kv_range(c_)
            nkt = j1 - j0
            if hp == 0:
                oi = cpos[c_]
                if oi + 1 < NCH:
                    tok_q[chunk_order[oi + 1] % 2] = q_load(chunk_order[oi + 1])
                xw = state["tok_x_free"][xh]
                state["tok_x"][xh] = [
                    QS.dma(AP(x_sb, xh * KT * C + p2 * C, [[2 * KT * C, 128], [1, C]]),
                           AP(x_in, p2 * 128 * S + c_ * C, [[S, 128], [1, C]]),
                           waits=[xw])
                    for p2 in range(KT)]
            tk1, tk2, tv1, tv2, nh_, buf = tok_kv.pop((c_, hp))
            dq_pending = None
            if dq_slices and idx >= 16:
                dq_pending = dq_slices.pop(0)
                dq_pending[0]()
            pvb = ps_pv[par]
            qk_tok = {}
            if pending_qk is not None:
                qk_tok[0] = pending_qk
                pending_qk = None
            else:
                qk_tok[0] = emit_qk(c_, hp, 0, tk1, buf)
            last_pvden = None
            for i in range(nkt):
                j = j0 + i
                sp = i % 2
                pp = ucnt[0] % 3
                ucnt[0] += 1
                d = j - 4 * c_
                if 0 <= d < 4:
                    te = kb.op("A", lambda eng, b=ps_s[sp], sp2=pp:
                               eng.activation(p_sb[:, sp2 * 2 * C:(sp2 + 1) * 2 * C],
                                              b[:, :], AF.Exp),
                               waits=[qk_tok[i], p_free[pp]], inc=sA)
                    mt_ = kb.op("V", lambda eng, sp2=pp, dd=d:
                                eng.tensor_mul(
                                    p_sb[:, sp2 * 2 * C:(sp2 + 1) * 2 * C],
                                    p_sb[:, sp2 * 2 * C:(sp2 + 1) * 2 * C],
                                    mask_sb[:, dd * 2 * C:(dd + 1) * 2 * C]),
                                waits=[te, tok_mask], inc=sV)
                    exp_free[sp] = te
                    pv_wait = pv_wait_o = mt_
                else:
                    te = kb.op("A", lambda eng, b=ps_s[sp], sp2=pp:
                               eng.activation(p_sb[:, sp2 * 2 * C:(sp2 + 1) * 2 * C],
                                              b[:, :], AF.Exp),
                               waits=[qk_tok[i], p_free[pp]], inc=sA)
                    exp_free[sp] = te
                    pv_wait = pv_wait_o = te
                if i + 1 < nkt:
                    qk_tok[i + 1] = emit_qk(c_, hp, i + 1,
                                            tk1 if i + 1 < nh_ else tk2, buf)
                elif idx + 1 < len(pairs):
                    nc3, nh3 = pairs[idx + 1]
                    ntk1, _, _, _, _, nbuf = tok_kv[(nc3, nh3)]
                    pending_qk = emit_qk(nc3, nh3, 0, ntk1, nbuf)
                w1 = [pv_wait, tv1 if i < nh_ else tv2]
                if i == 0:
                    w1 += [norm_free[par], state["db_free"][par]]
                    if par == 1 and variant == "a":
                        w1.append(state["wo_bank_free"])
                i0, il = (i == 0), (i == nkt - 1)
                kb.op("T", lambda eng, b=pvb, vv=vbuf[buf], ii=i, sp2=pp, a=i0, z=il:
                      eng.matmul(b[0:64, :], vv[:, ii * 130:ii * 130 + 64],
                                 p_sb[:, sp2 * 2 * C:sp2 * 2 * C + C],
                                 start=a, stop=z, tile_position=(0, 0)),
                      waits=w1)
                kb.op("T", lambda eng, b=pvb, vv=vbuf[buf], ii=i, sp2=pp, a=i0, z=il:
                      eng.matmul(b[64:128, :], vv[:, ii * 130 + 65:ii * 130 + 129],
                                 p_sb[:, sp2 * 2 * C + C:sp2 * 2 * C + 2 * C],
                                 start=a, stop=z, tile_position=(0, 64)),
                      waits=[pv_wait_o])
                kb.op("T", lambda eng, b=ps_db[par], vv=vbuf[buf], ii=i, sp2=pp, a=i0,
                      z=il:
                      eng.matmul(b[0:1, :], vv[:, ii * 130 + 64:ii * 130 + 65],
                                 p_sb[:, sp2 * 2 * C:sp2 * 2 * C + C],
                                 start=a, stop=z, tile_position=(0, 0)),
                      waits=[db_fill[par]] if i == 0 else [])
                last_pvden = kb.op("T", lambda eng, b=ps_db[par], vv=vbuf[buf], ii=i,
                                   sp2=pp, a=i0, z=il:
                                   eng.matmul(b[32:33, :],
                                              vv[:, ii * 130 + 129:ii * 130 + 130],
                                              p_sb[:, sp2 * 2 * C + C:
                                                   sp2 * 2 * C + 2 * C],
                                              start=a, stop=z,
                                              tile_position=(0, 32)),
                                   inc=sT)
                p_free[pp] = last_pvden
                if i in (3, 4) and deferred:
                    deferred.pop(0)()
                elif (i >= 5 and i % 2 == 1 and par == 0 and not deferred
                      and def_wo):
                    def_wo.pop(0)()
            while deferred:
                deferred.pop(0)()
            kv_free[(c_, hp)] = (qk_tok[nkt - 1], last_pvden)
            if hp == NP - 1:
                tok_q_last[c_] = qk_tok[nkt - 1]
            if idx + 5 < len(pairs):
                load_kv(*pairs[idx + 5], buf, (c_, hp))
            deferred.append(lambda c2=c_, h2=hp, p2=par, x2=xh, b2=pvb,
                            r2=last_pvden: emit_tail(c2, h2, p2, x2, b2, r2))
            if dq_pending is not None:
                dq_pending[1]()
            if variant == "a" and hp == NP - 1:
                if def_wo:
                    while deferred:
                        deferred.pop(0)()
                    while def_wo:
                        def_wo.pop(0)()
                deferred.append(lambda: state.__setitem__(
                    "wo_norm_toks", [norm_free[0], norm_free[1]]))
                for it in range(KT):
                    def_wo.append(lambda c2=c_, i2=it, x2=xh:
                                  emit_wo_group(c2, i2, x2))
            if idx == len(pairs) - 1:
                while deferred:
                    deferred.pop(0)()
                while def_wo:
                    def_wo.pop(0)()
        kb.emit_block(f"attend_{variant}")
        seg.close()

    def emit_ffn(first_call):
        out_dram = xres if first_call else outT
        seg = ExitStack()
        pfx = "f1" if first_call else "f2"
        s_ = lambda name, shape, dt: seg.enter_context(
            nc.sbuf_tensor(f"{pfx}_{name}", shape, dt))
        ps_ = lambda name, shape: seg.enter_context(
            nc.psum_tensor(f"{pfx}_ps_{name}", shape, F32))
        w1_sb = s_("w1", [128, KT * FF], BF16)
        w2_sb = s_("w2", [128, 32 * D], BF16)
        ps_sx = ps_("sx", [1, C])
        ps_sq = ps_("sq", [1, C])
        ps_r = ps_("r", [128, C])
        ps_m = ps_("m", [128, C])
        ps_f1 = [ps_("f1a", [128, HC]), ps_("f1b", [128, HC])]
        ps_f2 = [ps_("f2a", [128, HC]), ps_("f2b", [128, HC])]

        w1_tok = [None] * KT
        w2_tok = [None] * 32

        x_sb = s_("x", [128, KT * C], F32)
        x_bf = s_("xbf", [128, KT * C], BF16)
        y_sb = s_("y", [128, KT * C], BF16)
        h_sb = s_("h", [128, KT * C], BF16)
        a_sb = s_("a", [128, 32 * HC], BF16)
        t2f_buf = s_("t2f", [128, KT * C], BF16)
        sq_buf = s_("sqb", [128, 2 * C], BF16)
        rb_r = s_("rbr", [128, C], BF16)
        rb_m = s_("rbm", [128, C], BF16)
        t1_buf = s_("t1", [128, C], BF16)
        z_sb = s_("z", [128, 2 * HC], F32)
        stage = s_("stage", [128, 2 * HC], F32)
        st = {"free_sx": None, "free_sq": None, "free_r": None, "free_m": None,
              "xbf_free": None, "y_used": None, "sum_toks": None, "row_tok": None,
              "xbf_tok": None, "h_used": None, "x_free": None, "x_tok": None,
              "xy_done": None, "h_aff": {}, "ivt": None, "rb_used": None}

        def xbf_load(c_, sliced=False):
            if sliced:
                st["xbf_tok"] = [
                    QP.dma(AP(x_bf, k2 * C, [[KT * C, 128], [1, C]]),
                           AP(xres, k2 * 128 * S + c_ * C, [[S, 128], [1, C]]),
                           waits=[st["xbf_free"]])
                    for k2 in range(KT)]
            else:
                t_ = QP.dma(
                    AP(x_bf, 0, [[KT * C, 128], [C, KT], [1, C]]),
                    AP(xres, c_ * C, [[S, 128], [128 * S, KT], [1, C]]),
                    waits=[st["xbf_free"]])
                st["xbf_tok"] = [t_] * KT

        def x_load(c_):
            st["x_tok"] = QS.dma(
                AP(x_sb, 0, [[KT * C, 128], [C, KT], [1, C]]),
                AP(xres, c_ * C, [[S, 128], [128 * S, KT], [1, C]]),
                waits=[st["x_free"]])

        def ln_stats(src_sb, src_tok, extra=None):
            tks = src_tok if isinstance(src_tok, list) else [src_tok] * KT
            t1 = t2 = None
            buf_tok = {}
            for kt in range(KT):
                cs = kt * C
                par = kt % 2
                sqt = kb.op("V", lambda eng, o=cs, p=par, sb2=src_sb:
                            eng.tensor_mul(sq_buf[:, p * C:(p + 1) * C],
                                           sb2[:, o:o + C], sb2[:, o:o + C]),
                            waits=[tks[kt], buf_tok.get(par)], inc=sV)
                t1 = kb.op("T", lambda eng, k=kt, o=cs, sb2=src_sb:
                           eng.matmul(ps_sx[0:1, :], ones_col[:, :], sb2[:, o:o + C],
                                      start=(k == 0), stop=(k == KT - 1)),
                           waits=[tks[kt]] + ([st["free_sx"]] if kt == 0 else []),
                           inc=sT)
                t2 = kb.op("T", lambda eng, k=kt, p=par:
                           eng.matmul(ps_sq[0:1, :], ones_col[:, :],
                                      sq_buf[:, p * C:(p + 1) * C],
                                      start=(k == 0), stop=(k == KT - 1)),
                           waits=[sqt] + ([st["free_sq"]] if kt == 0 else []), inc=sT)
                buf_tok[par] = t2
            st["sum_toks"] = (t1, t2)

        def ln_rows():
            t1, t2 = st["sum_toks"]
            m1 = kb.op("V", lambda eng: eng.tensor_scalar_mul(r_mean[:, :],
                                                              ps_sx[0:1, :], 1.0 / D),
                       waits=[t1], inc=sV)
            kb.op("V", lambda eng: eng.tensor_mul(r_var[:, :], r_mean[:, :],
                                                  r_mean[:, :]), inc=sV)
            vt = kb.op("V", lambda eng: eng.scalar_tensor_tensor(
                r_var[:, :], ps_sq[0:1, :], 1.0 / D, r_var[:, :],
                ALU.mult, ALU.subtract), waits=[t2], inc=sV)
            st["free_sx"], st["free_sq"] = m1, vt
            kb.op("A", lambda eng: eng.activation(r_rstd[:, :], r_var[:, :], AF.Ln,
                                                  bias=consts[0:1, EPS_C:EPS_C + 1],
                                                  scale=1.0),
                  waits=[vt], inc=sA)
            at = kb.op("A", lambda eng: eng.activation(r_rstd[:, :], r_rstd[:, :],
                                                       AF.Exp, bias=0.0,
                                                       scale=consts[0:1, NEGH_C:NEGH_C + 1]),
                       inc=sA)
            kb.op("V", lambda eng: eng.tensor_mul(r_mrs[:, :], r_mean[:, :],
                                                  r_rstd[:, :]),
                  waits=[at], inc=sV)
            kb.op("V", lambda eng: eng.tensor_copy(r_rstd_bf[:, :], r_rstd[:, :]),
                  waits=[at], inc=sV)
            st["row_tok"] = kb.op("V", lambda eng: eng.tensor_copy(r_mrs_bf[:, :],
                                                                   r_mrs[:, :]), inc=sV)

        def ln_bcast():
            b1 = kb.op("T", lambda eng: eng.matmul(ps_r[:, :], ones_row[:, :],
                                                   r_rstd_bf[:, :],
                                                   start=True, stop=True),
                       waits=[st["row_tok"], st["free_r"]], inc=sT)
            b2 = kb.op("T", lambda eng: eng.matmul(ps_m[:, :], ones_row[:, :],
                                                   r_mrs_bf[:, :],
                                                   start=True, stop=True),
                       waits=[st["free_m"]], inc=sT)
            return b1, b2

        def oln_norm(bc, src_tok, gate):
            tks = src_tok if isinstance(src_tok, list) else [src_tok] * KT
            b1, b2 = bc
            e1 = kb.op("V", lambda eng: eng.tensor_copy(rb_r[:, :], ps_r[:, :]),
                       waits=[b1, st["rb_used"]], inc=sV)
            e2 = kb.op("V", lambda eng: eng.tensor_copy(rb_m[:, :], ps_m[:, :]),
                       waits=[b2], inc=sV)
            st["free_r"], st["free_m"] = e1, e2
            out_tok = None
            for kt in range(KT):
                cs = kt * C
                kb.op("V", lambda eng, o=cs, k=kt:
                      eng.tensor_tensor(t1_buf[:, 0:C],
                                        x_bf[:, o:o + C], rb_r[:, :], ALU.mult),
                      waits=[e1, tks[kt], gate], inc=sV)
                out_tok = kb.op("V", lambda eng, o=cs:
                                eng.tensor_tensor(y_sb[:, o:o + C],
                                                  t1_buf[:, 0:C],
                                                  rb_m[:, :], ALU.subtract),
                                waits=[e2], inc=sV)
            st["rb_used"] = out_tok
            st["xbf_free"] = out_tok
            return out_tok

        def iln_norm_v(bc, ytok):
            b1, b2 = bc
            e1 = kb.op("V", lambda eng: eng.tensor_copy(rb_r[:, :], ps_r[:, :]),
                       waits=[b1, st["rb_used"]], inc=sV)
            e2 = kb.op("V", lambda eng: eng.tensor_copy(rb_m[:, :], ps_m[:, :]),
                       waits=[b2], inc=sV)
            st["free_r"], st["free_m"] = e1, e2
            toks = []
            for kt in range(KT):
                cs = kt * C
                kb.op("V", lambda eng, o=cs:
                      eng.tensor_tensor(t1_buf[:, 0:C],
                                        y_sb[:, o:o + C], rb_r[:, :], ALU.mult),
                      waits=[e1, ytok], inc=sV)
                toks.append(kb.op("V", lambda eng, o=cs:
                                  eng.tensor_tensor(t2f_buf[:, o:o + C],
                                                    t1_buf[:, 0:C],
                                                    rb_m[:, :], ALU.subtract),
                                  waits=[e2, st["h_aff"].get(kt)], inc=sV))
            st["rb_used"] = toks[-1]
            return toks

        def iln_affine(kt, vtoks, gate):
            cs = kt * C
            tok = kb.op("A", lambda eng, o=cs, k=kt:
                        eng.activation(h_sb[:, o:o + C], t2f_buf[:, o:o + C],
                                       AF.Identity, bias=bet[:, k:k + 1],
                                       scale=gam[:, k:k + 1]),
                        waits=[vtoks[kt], gate], inc=sA)
            st["h_aff"][kt] = tok
            return tok

        xbf_load(0, sliced=True)

        Q1 = KT * FF // 4
        w1t = []
        for qi in range(4):
            w1t.append(QP.dma(AP(w1_sb, qi * Q1, [[KT * FF, 128], [1, Q1]]),
                              AP(w1scr, qi * Q1, [[KT * FF, 128], [1, Q1]])))
        H2 = 32 * D // 2
        t2a = QP.dma(AP(w2_sb, 0, [[32 * D, 128], [1, H2]]),
                     AP(w2scr, 0, [[32 * D, 128], [1, H2]]))
        t2b = QP.dma(AP(w2_sb, H2, [[32 * D, 128], [1, H2]]),
                     AP(w2scr, H2, [[32 * D, 128], [1, H2]]))
        for i in range(KT):
            w1_tok[i] = w1t[i // 2]
        for i in range(32):
            w2_tok[i] = t2a if i < 16 else t2b

        ln_stats(x_bf, st["xbf_tok"])
        ln_rows()
        ytok = oln_norm(ln_bcast(), st["xbf_tok"], None)
        xbf_load(1)
        ln_stats(y_sb, ytok)
        ln_rows()
        _ivt = iln_norm_v(ln_bcast(), ytok)
        htok = None
        for _k in range(KT):
            htok = iln_affine(_k, _ivt, None)

        a_free = {}
        silu_toks = {}
        z_free = {}
        stage_dma = []
        for c_ in range(NCH):
            x_load(c_)
            my_y = ytok
            xy_tok = None
            my_h = htok
            last_stage = None
            nx = c_ + 1 < NCH
            for hf in range(2):
                for nt in range(32):
                    bank = ps_f1[nt % 2]
                    free = silu_toks.get((1 - hf, 30 + nt)) if nt < 2 else                         silu_toks.get((hf, nt - 2))
                    mt = None
                    for kt in range(KT):
                        mt = kb.op("T", lambda eng, b=bank, n=nt, k=kt, h=hf:
                                   eng.matmul(b[:, :],
                                              w1_sb[:, k * FF + n * 128:
                                                    k * FF + (n + 1) * 128],
                                              h_sb[:, k * C + h * HC:k * C + h * HC + HC],
                                              start=(k == 0), stop=(k == KT - 1)),
                                   waits=([my_h, w1_tok[kt], free, a_free.get(nt)]
                                          if kt == 0 else [w1_tok[kt]]),
                                   inc=sT if kt == KT - 1 else None)
                    silu_toks[(hf, nt)] = kb.op(
                        "A", lambda eng, b=bank, n=nt:
                        eng.activation(a_sb[:, n * HC:(n + 1) * HC], b[:, :], AF.Silu,
                                       bias=fc1b[:, n:n + 1], scale=1.0),
                        waits=[mt], inc=sA)
                    if hf == 0:
                        if nx and nt == 4:
                            ln_stats(x_bf, st["xbf_tok"])
                        elif nx and nt == 12:
                            ln_rows()
                        elif nt == 22:
                            for kt2 in range(KT):
                                xy_tok = kb.op(
                                    "V", lambda eng, o=kt2 * C:
                                    eng.tensor_tensor(x_sb[:, o:o + C],
                                                      x_sb[:, o:o + C],
                                                      y_sb[:, o:o + C], ALU.add),
                                    waits=[st["x_tok"], my_y,
                                           st["xy_done"] if kt2 == 0 else None],
                                    inc=sV)
                            st["y_used"] = xy_tok
                        elif nx and nt == 28:
                            ytok = oln_norm(ln_bcast(), st["xbf_tok"], st["y_used"])
                    if nx and hf == 1 and nt == 16:
                        st["ibc"] = ln_bcast()
                if hf == 1 and nx:
                    st["ivt"] = iln_norm_v(st["ibc"], ytok)
                    fc1_last = mt
                for jt in range(KT):
                    bank = ps_f2[jt % 2]
                    zfree = z_free.get(jt % 2)
                    mt = None
                    for nt in range(32):
                        mt = kb.op("T", lambda eng, b=bank, n=nt, j=jt:
                                   eng.matmul(b[:, :],
                                              w2_sb[:, n * D + j * 128:
                                                    n * D + (j + 1) * 128],
                                              a_sb[:, n * HC:(n + 1) * HC],
                                              start=(n == 0), stop=(n == 31)),
                                   waits=[silu_toks[(hf, nt)], w2_tok[nt]] +
                                   ([zfree] if nt == 0 else []),
                                   inc=sT if nt == 31 else None)
                        if jt == KT - 1:
                            a_free[nt] = mt
                    zt = kb.op("A", lambda eng, b=bank, p=jt % 2, j=jt:
                               eng.activation(z_sb[:, p * HC:(p + 1) * HC], b[:, :],
                                              AF.Identity, bias=fc2bh[:, j:j + 1],
                                              scale=consts[:, HALF_C:HALF_C + 1]),
                               waits=[mt, z_free.get(("t", jt % 2))], inc=sA)
                    z_free[jt % 2] = zt
                    war = stage_dma[-2] if len(stage_dma) >= 2 else None
                    st_ = kb.op("V", lambda eng, p=jt % 2, j=jt, h=hf:
                                eng.tensor_tensor(stage[:, p * HC:(p + 1) * HC],
                                                  z_sb[:, p * HC:(p + 1) * HC],
                                                  x_sb[:, j * C + h * HC:
                                                       j * C + h * HC + HC],
                                                  ALU.add),
                                waits=[zt, xy_tok, war], inc=sV)
                    z_free[("t", jt % 2)] = st_
                    stage_dma.append(QO.dma(
                        AP(out_dram, jt * 128 * S + c_ * C + hf * HC,
                           [[S, 128], [1, HC]]),
                        stage[:, (jt % 2) * HC:(jt % 2 + 1) * HC], waits=[st_]))
                    last_stage = st_
                    if nx and hf == 0 and jt == 3:
                        ln_stats(y_sb, ytok)
                    elif nx and hf == 0 and jt == 6:
                        ln_rows()
                    if nx and hf == 1:
                        htok = iln_affine(jt, st["ivt"], fc1_last)
            if nx and c_ + 2 < NCH:
                xbf_load(c_ + 2)
            st["x_free"] = last_stage
            st["xy_done"] = last_stage
        kb.emit_block("ffn1" if first_call else "ffn2")
        seg.close()

    def emit_debug_copy():
        for kt in range(KT):
            QS.dma(AP(outT, kt * 128 * S, [[S, 128], [1, S]]),
                   AP(xres, kt * 128 * S, [[S, 128], [1, S]]))
        kb.emit_block("dbgcopy")

    emit_p0()
    if PH == 16:
        for kt in range(KT):
            QS.dma(AP(xres, kt * 128 * S, [[S, 128], [1, S]]),
                   AP(xT, kt * 128 * S, [[S, 128], [1, S]]))
        kb.emit_block("dbgseed")
        emit_prep("b")
        emit_attend("b")
        emit_debug_copy()
        stack.close()
        return nc
    if PH >= 1:
        emit_prep("a")
    if PH >= 2:
        emit_attend("a")
    if PH >= 4:
        emit_ffn(True)
    if PH >= 5:
        emit_prep("b")
    if PH >= 6:
        emit_attend("b")
    if PH >= 7:
        emit_ffn(False)
    else:
        emit_debug_copy()
    stack.close()
    return nc


def _prep_inputs(inputs):
    bf = ml_dtypes.bfloat16
    g = {}
    g["wqaT"] = np.ascontiguousarray(inputs["L0_attn_a_Wq"].T)
    g["wkaT"] = np.ascontiguousarray(inputs["L0_attn_a_Wk"].T)
    g["wvaT"] = np.ascontiguousarray(inputs["L0_attn_a_Wv"].T)
    g["woaT"] = np.ascontiguousarray(inputs["L0_attn_a_Wo"].T)
    g["wqb"] = np.ascontiguousarray(
        inputs["L0_attn_b_Wq"].transpose(1, 0, 2).reshape(D, D))
    g["wkb"] = np.ascontiguousarray(
        inputs["L0_attn_b_Wk"].transpose(1, 0, 2).reshape(D, D))
    g["bqb"] = np.ascontiguousarray(inputs["L0_attn_b_bq"].reshape(NP, 128).T)
    g["bkb"] = np.ascontiguousarray(inputs["L0_attn_b_bk"].reshape(NP, 128).T)
    g["wvdT"] = np.ascontiguousarray(
        inputs["L0_attn_b_Wvd"].transpose(2, 0, 1).reshape(16, 16 * D))
    g["wvu"] = np.ascontiguousarray(
        inputs["L0_attn_b_Wvu"].transpose(1, 0, 2).reshape(16, H * DH))
    g["bvdT"] = np.ascontiguousarray(inputs["L0_attn_b_bvd"].T)
    g["bvu_row"] = np.ascontiguousarray(inputs["L0_attn_b_bvu"].reshape(1, H * DH))
    g["fc1q"] = np.ascontiguousarray(inputs["L0_ffn_fc1_q"])
    g["s1rows"] = np.ascontiguousarray(inputs["L0_ffn_fc1_scales"].reshape(1, 8 * FF))
    g["fc1b"] = np.ascontiguousarray(inputs["L0_ffn_fc1_bias"].reshape(32, 128).T)
    g["fc2q"] = np.ascontiguousarray(inputs["L0_ffn_fc2_q"])
    g["s2rows"] = np.ascontiguousarray(inputs["L0_ffn_fc2_scales"].reshape(1, 32 * D))
    g["fc2b"] = np.ascontiguousarray(inputs["L0_ffn_fc2_bias"].reshape(8, 128).T)
    g["gam"] = np.ascontiguousarray(inputs["L0_ffn_ln_gamma"].reshape(8, 128).T)
    g["bet"] = np.ascontiguousarray(inputs["L0_ffn_ln_beta"].reshape(8, 128).T)
    i_idx = np.arange(128)[:, None]
    q_idx = np.arange(C)[None, :]
    ma = np.zeros((128, 4, 2, C), dtype=bf)
    mb = np.zeros((128, 4, 2, C), dtype=bf)
    for d in range(4):
        keep_a = ((128 * d + i_idx) <= q_idx)
        keep_b = ((128 * d + i_idx) >= q_idx)
        for h2 in range(2):
            ma[:, d, h2, :] = keep_a
            mb[:, d, h2, :] = keep_b
    g["maska"] = np.ascontiguousarray(ma.reshape(128, 4 * 2 * C))
    g["maskb"] = np.ascontiguousarray(mb.reshape(128, 4 * 2 * C))
    g["ones_col"] = np.ones((128, 1), dtype=bf)
    g["ones_row"] = np.ones((1, 128), dtype=bf)
    em = np.zeros((33, 128), dtype=bf)
    em[0, 0:64] = 1.0
    em[32, 64:128] = 1.0
    g["ematD"] = em
    return g


def kernel(**inputs):
    x = np.asarray(inputs["L0_attn_a_in_features"])
    B = x.shape[0]
    g = _prep_inputs(inputs)
    nc = build_program()
    in_maps = []
    for c in range(B):
        m = dict(g)
        m["xT"] = np.ascontiguousarray(x[c].T)
        in_maps.append(m)
    res = run_bass_kernel_spmd(nc, in_maps, core_ids=list(range(B)))
    out = np.empty((B, S, D), dtype=np.float32)
    for c in range(B):
        out[c] = np.asarray(res.results[c]["outT"]).T
    return out


# version 32
# speedup vs baseline: 1.0055x; 1.0055x over previous
import os
from contextlib import ExitStack

import numpy as np
import ml_dtypes

import concourse.bass as bass
import concourse.mybir as mybir
from concourse.bass_utils import run_bass_kernel_spmd

F32 = mybir.dt.float32
BF16 = mybir.dt.bfloat16
I32 = mybir.dt.int32
AF = mybir.ActivationFunctionType
ALU = mybir.AluOpType
AP = bass.AP

S, D, H, DH, FF = 4096, 1024, 16, 64, 4096
C = 512
HC = 256
NCH = S // C
KT = D // 128
NP = H // 2
NT130 = (S // 128) * 130
PH = int(os.environ.get("MK_PHASES", "99"))

EPS_C, NEGH_C, QSA_C, QSB_C, HALF_C, NEG1_C = 0, 1, 2, 3, 4, 5


class Sem:
    __slots__ = ("h", "n", "name")

    def __init__(self, nc, stack, name):
        self.h = stack.enter_context(nc.semaphore(name))
        self.n = 0
        self.name = name


class Kb:
    ENGINES = ("T", "V", "A", "P", "S")
    ENGMAP = {"T": "tensor", "V": "vector", "A": "scalar", "P": "gpsimd", "S": "sync"}

    def __init__(self, nc, stack):
        self.nc = nc
        self.stack = stack
        self.ops = []
        self.watermark = {e: {} for e in self.ENGINES}
        self.queues = []

    def sem(self, name):
        return Sem(self.nc, self.stack, name)

    def op(self, eng, fn, waits=(), inc=None, n=1):
        tok = None
        w = [x for x in waits if x is not None]
        if inc is not None:
            inc.n += n
            tok = (inc, inc.n)
        self.ops.append((eng, w, fn, (inc, n) if inc is not None else None))
        return tok

    def wait(self, eng, toks):
        self.ops.append((eng, [t for t in toks if t is not None], None, None))

    def emit_block(self, name):
        for q in self.queues:
            if q.dirty:
                self.wait(q.eng_name, [(s, s.n) for s in q.sems if s.n > 0])
                q.dirty = False
        ops = self.ops
        self.ops = []
        used = {e for (e, _, _, _) in ops}
        with self.nc.Block(name) as block:
            for ename in self.ENGINES:
                if ename not in used:
                    continue

                def runner(eng, ename=ename, ops=ops):
                    wm = self.watermark[ename]
                    for (e, waits, fn, inc) in ops:
                        if e != ename:
                            continue
                        for (sem, val) in waits:
                            if wm.get(id(sem), -1) >= val:
                                continue
                            eng.wait_ge(sem.h, val)
                            wm[id(sem)] = val
                        if fn is None:
                            continue
                        ins = fn(eng)
                        if inc is not None:
                            sem, n = inc
                            ins.then_inc(sem.h, n)
                getattr(block, self.ENGMAP[ename])(runner)


class DmaQ:
    def __init__(self, kb, eng_name, name, depth=8):
        self.kb = kb
        self.eng_name = eng_name
        self.sems = [kb.sem(f"{name}{i}") for i in range(depth)]
        self.i = 0
        self.dirty = False
        kb.queues.append(self)

    def dma(self, dst, src, waits=()):
        s = self.sems[self.i]
        self.i = (self.i + 1) % len(self.sems)
        prewait = (s, s.n) if s.n > 0 else None
        self.dirty = True
        return self.kb.op(
            self.eng_name,
            lambda eng, d=dst, so=src: eng.dma_start(d, so),
            waits=[*waits, prewait], inc=s, n=16)


def build_program():
    nc = bass.Bass("TRN2", target_bir_lowering=False)

    din = lambda name, shape, dt: nc.dram_tensor(name, shape, dt, kind="ExternalInput")
    xT = din("xT", [D, S], F32)
    wqaT = din("wqaT", [D, D], F32)
    wkaT = din("wkaT", [D, D], F32)
    wvaT = din("wvaT", [D, D], F32)
    woaT = din("woaT", [D, D], F32)
    wqb_d = din("wqb", [D, D], F32)
    wkb_d = din("wkb", [D, D], F32)
    bqb_d = din("bqb", [128, NP], F32)
    bkb_d = din("bkb", [128, NP], F32)
    wvdT_d = din("wvdT", [16, 16 * D], F32)
    wvu_d = din("wvu", [16, H * DH], F32)
    bvdT_d = din("bvdT", [16, 16], F32)
    bvu_d = din("bvu_row", [1, H * DH], F32)
    fc1q = din("fc1q", [D, FF], I32)
    s1rows = din("s1rows", [1, 8 * FF], F32)
    fc1b_d = din("fc1b", [128, 32], F32)
    fc2q = din("fc2q", [FF, D], I32)
    s2rows = din("s2rows", [1, 32 * D], F32)
    fc2b_d = din("fc2b", [128, 8], F32)
    gam_d = din("gam", [128, 8], F32)
    bet_d = din("bet", [128, 8], F32)
    maska_d = din("maska", [128, 4 * 2 * C], BF16)
    maskb_d = din("maskb", [128, 4 * 2 * C], BF16)
    ones_col_d = din("ones_col", [128, 1], BF16)
    ones_row_d = din("ones_row", [1, 128], BF16)
    ematD_d = din("ematD", [33, 128], BF16)

    outT = nc.dram_tensor("outT", [D, S], F32, kind="ExternalOutput")

    xres = nc.dram_tensor("xres", [D, S], F32, kind="Internal")
    qscr = nc.dram_tensor("qscr", [128, NP * S], BF16, kind="Internal")
    kscr = nc.dram_tensor("kscr", [128, NP * S], BF16, kind="Internal")
    vscr = nc.dram_tensor("vscr", [128, NP * NT130], BF16, kind="Internal")
    w1scr = nc.dram_tensor("w1scr", [128, KT * FF], BF16, kind="Internal")
    w2scr = nc.dram_tensor("w2scr", [128, 32 * D], BF16, kind="Internal")

    stack = ExitStack()
    stack.enter_context(nc.allow_low_precision("bf16 matmuls per problem tolerance spec"))
    kb = Kb(nc, stack)
    sT = kb.sem("sT")
    sV = kb.sem("sV")
    sA = kb.sem("sA")
    sP = kb.sem("sP")
    QS = DmaQ(kb, "S", "qs", 8)
    QP = DmaQ(kb, "P", "qp", 8)
    QO = DmaQ(kb, "S", "qo", 4)

    sb = lambda name, shape, dt: stack.enter_context(nc.sbuf_tensor(name, shape, dt))

    consts = sb("consts", [128, 8], F32)
    ones_col = sb("g_ones_col", [128, 1], BF16)
    ones_row = sb("g_ones_row", [1, 128], BF16)
    ematD = sb("g_ematD", [33, 128], BF16)
    bqbs = sb("g_bqbs", [128, NP], F32)
    bkb = sb("g_bkb", [128, NP], F32)
    fc1b = sb("g_fc1b", [128, 32], F32)
    fc2bh = sb("g_fc2bh", [128, 8], F32)
    gam = sb("g_gam", [128, 8], F32)
    bet = sb("g_bet", [128, 8], F32)
    r_mean = sb("g_rmean", [1, C], F32)
    r_var = sb("g_rvar", [1, C], F32)
    r_rstd = r_var
    r_mrs = r_mean
    r_rstd_bf = sb("g_rrstdbf", [1, C], BF16)
    r_mrs_bf = sb("g_rmrsbf", [1, C], BF16)

    def emit_p0():
        for col, val in ((EPS_C, 1e-5), (NEGH_C, -0.5), (QSA_C, 0.125),
                         (QSB_C, 1.0 / 32.0), (HALF_C, 0.5),
                         (NEG1_C, -1.0)):
            kb.op("P", lambda eng, c=col, v=val: eng.memset(consts[:, c:c + 1], v), inc=sP)
        t_bq = QS.dma(bqbs[:, :], bqb_d.ap())
        QS.dma(bkb[:, :], bkb_d.ap())
        t_b2 = QS.dma(fc2bh[:, :], fc2b_d.ap())
        QS.dma(fc1b[:, :], fc1b_d.ap())
        QS.dma(gam[:, :], gam_d.ap())
        QS.dma(bet[:, :], bet_d.ap())
        QP.dma(ones_col[:, :], ones_col_d.ap())
        QP.dma(ones_row[:, :], ones_row_d.ap())
        QP.dma(ematD[:, :], ematD_d.ap())
        kb.op("V", lambda eng: eng.tensor_scalar_mul(bqbs[:, :], bqbs[:, :], 1.0 / 32.0),
              waits=[t_bq], inc=sV)
        kb.op("V", lambda eng: eng.tensor_scalar_mul(fc2bh[:, :], fc2bh[:, :], 0.5),
              waits=[t_b2], inc=sV)
        kb.emit_block("p0")

    def ln_chunk(ctx, src, src_is_f32, x_tok, war_tok, out_bf, affine=None,
                 out_aff=None):
        ps_sx, ps_sq, ps_r, ps_m, cast_buf, sq_buf, t1_buf, st = ctx
        t1 = t2 = None
        buf_tok = {}
        for kt in range(KT):
            cs = kt * C
            par = kt % 2
            if src_is_f32:
                ct = kb.op("V", lambda eng, o=cs, p=par:
                           eng.tensor_copy(cast_buf[:, p * C:(p + 1) * C], src[:, o:o + C]),
                           waits=[x_tok, war_tok, buf_tok.get(("c", par))], inc=sV)
                rhs = lambda p=par: cast_buf[:, p * C:(p + 1) * C]
            else:
                ct = x_tok
                rhs = lambda o=cs: src[:, o:o + C]
            sqt = kb.op("V", lambda eng, p=par, r=rhs:
                        eng.tensor_mul(sq_buf[:, p * C:(p + 1) * C], r(), r()),
                        waits=[ct, war_tok, buf_tok.get(("s", par))], inc=sV)
            t1 = kb.op("T", lambda eng, k=kt, r=rhs:
                       eng.matmul(ps_sx[0:1, :], ones_col[:, :], r(),
                                  start=(k == 0), stop=(k == KT - 1)),
                       waits=[ct] + ([st["free_sx"]] if kt == 0 else []), inc=sT)
            t2 = kb.op("T", lambda eng, k=kt, p=par:
                       eng.matmul(ps_sq[0:1, :], ones_col[:, :],
                                  sq_buf[:, p * C:(p + 1) * C],
                                  start=(k == 0), stop=(k == KT - 1)),
                       waits=[sqt] + ([st["free_sq"]] if kt == 0 else []), inc=sT)
            buf_tok[("c", par)] = t1
            buf_tok[("s", par)] = t2
        m1 = kb.op("V", lambda eng: eng.tensor_scalar_mul(r_mean[:, :], ps_sx[0:1, :],
                                                          1.0 / D),
                   waits=[t1], inc=sV)
        m2 = kb.op("V", lambda eng: eng.tensor_scalar_mul(r_msq[:, :], ps_sq[0:1, :],
                                                          1.0 / D),
                   waits=[t2], inc=sV)
        st["free_sx"], st["free_sq"] = m1, m2
        kb.op("V", lambda eng: eng.tensor_mul(r_var[:, :], r_mean[:, :], r_mean[:, :]),
              inc=sV)
        vt = kb.op("V", lambda eng: eng.tensor_sub(r_var[:, :], r_msq[:, :], r_var[:, :]),
                   inc=sV)
        kb.op("A", lambda eng: eng.activation(r_rstd[:, :], r_var[:, :], AF.Ln,
                                              bias=consts[0:1, EPS_C:EPS_C + 1],
                                              scale=1.0),
              waits=[vt], inc=sA)
        at = kb.op("A", lambda eng: eng.activation(r_rstd[:, :], r_rstd[:, :], AF.Exp,
                                                   bias=0.0,
                                                   scale=consts[0:1, NEGH_C:NEGH_C + 1]),
                   inc=sA)
        kb.op("V", lambda eng: eng.tensor_mul(r_mrs[:, :], r_mean[:, :], r_rstd[:, :]),
              waits=[at], inc=sV)
        kb.op("V", lambda eng: eng.tensor_copy(r_rstd_bf[:, :], r_rstd[:, :]),
              waits=[at], inc=sV)
        rt = kb.op("V", lambda eng: eng.tensor_copy(r_mrs_bf[:, :], r_mrs[:, :]), inc=sV)
        b1 = kb.op("T", lambda eng: eng.matmul(ps_r[:, :], ones_row[:, :], r_rstd_bf[:, :],
                                               start=True, stop=True),
                   waits=[rt, st["free_r"]], inc=sT)
        b2 = kb.op("T", lambda eng: eng.matmul(ps_m[:, :], ones_row[:, :], r_mrs_bf[:, :],
                                               start=True, stop=True),
                   waits=[st["free_m"]], inc=sT)
        out_tok = None
        for kt in range(KT):
            cs = kt * C
            par = kt % 2
            kb.op("V", lambda eng, o=cs, p=par:
                  eng.tensor_tensor(t1_buf[:, p * C:(p + 1) * C], src[:, o:o + C],
                                    ps_r[:, :], ALU.mult),
                  waits=[b1, x_tok, war_tok], inc=sV)
            if affine is None:
                out_tok = kb.op("V", lambda eng, o=cs, p=par:
                                eng.tensor_tensor(out_bf[:, o:o + C],
                                                  t1_buf[:, p * C:(p + 1) * C],
                                                  ps_m[:, :], ALU.subtract),
                                waits=[b2], inc=sV)
            else:
                g_sb, b_sb = affine
                nrm = kb.op("V", lambda eng, p=par:
                            eng.tensor_tensor(out_bf[:, p * C:(p + 1) * C],
                                              t1_buf[:, p * C:(p + 1) * C],
                                              ps_m[:, :], ALU.subtract),
                            waits=[b2, out_tok], inc=sV)
                out_tok = kb.op("A", lambda eng, o=cs, k=kt, p=par:
                                eng.activation(out_aff[:, o:o + C],
                                               out_bf[:, p * C:(p + 1) * C], AF.Identity,
                                               bias=b_sb[:, k:k + 1],
                                               scale=g_sb[:, k:k + 1]),
                                waits=[nrm], inc=sA)
        st["free_r"], st["free_m"] = out_tok, out_tok
        return out_tok

    def wload(dst_sb, src_dram, q):
        dst = AP(dst_sb, 0, [[KT * D, 128], [D, KT], [1, D]])
        src = AP(src_dram, 0, [[D, 128], [128 * D, KT], [1, D]])
        return q.dma(dst, src)

    def emit_prep(variant):
        seg = ExitStack()
        pfx = f"pr{variant}"
        s_ = lambda name, shape, dt: seg.enter_context(
            nc.sbuf_tensor(f"{pfx}_{name}", shape, dt))
        ps_ = lambda name, shape: seg.enter_context(
            nc.psum_tensor(f"{pfx}_ps_{name}", shape, F32))
        wq_l = s_("wq", [128, KT * D], BF16)
        wk_l = s_("wk", [128, KT * D], BF16)
        wv_m = s_("wv", [128, KT * D], BF16)
        x_bf = s_("xbf", [128, KT * C], BF16)
        y_sb = s_("y", [128, 2 * KT * C], BF16)
        qch = s_("qch", [128, KT * C], BF16)
        kch = s_("kch", [128, KT * C], BF16)
        vch = s_("vch", [128, NP * 520], BF16)
        sq_buf = s_("sq", [128, 2 * C], BF16)
        t1_buf = s_("t1", [128, C], BF16)
        rb_r = s_("rbr", [128, C], BF16)
        rb_m = s_("rbm", [128, C], BF16)
        ps_sx = ps_("sx", [1, C])
        ps_sq = ps_("sq", [1, C])
        ps_r = ps_("r", [128, C])
        ps_m = ps_("m", [128, C])
        ps_qk = [ps_("qk0", [128, C]), ps_("qk1", [128, C])]
        ps_v = [ps_("v0", [128, C]), ps_("v1", [128, C])]
        st = {"free_sx": None, "free_sq": None, "free_r": None, "free_m": None,
              "xbf_free": None, "y_used": [None, None], "sum_toks": None,
              "row_tok": None, "xtok": None, "norm_done": [None] * (NCH + 1),
              "rb_used": None}
        corder = (list(range(NCH)) if variant == "a"
                  else list(range(NCH - 1, -1, -1)))
        x_src = xT if variant == "a" else xres

        def x_load(c_, sliced=False):
            if sliced:
                st["xtok"] = [
                    QP.dma(AP(x_bf, k2 * C, [[KT * C, 128], [1, C]]),
                           AP(x_src, k2 * 128 * S + c_ * C, [[S, 128], [1, C]]),
                           waits=[st["xbf_free"]])
                    for k2 in range(KT)]
            else:
                t_ = QP.dma(
                    AP(x_bf, 0, [[KT * C, 128], [C, KT], [1, C]]),
                    AP(x_src, c_ * C, [[S, 128], [128 * S, KT], [1, C]]),
                    waits=[st["xbf_free"]])
                st["xtok"] = [t_] * KT

        x_load(corder[0], sliced=True)

        if variant == "a":
            tw_q = wload(wq_l, wqaT, QP)
            tw_k = wload(wk_l, wkaT, QP)
            tw_v = wload(wv_m, wvaT, QP)
            biasb_sb = None
        else:
            wvdt_sb = s_("wvdt", [16, 16 * D], BF16)
            wvu_sb = s_("wvu", [16, H * DH], BF16)
            bvdT_sb = s_("bvdT", [16, 16], BF16)
            bvu_sb = s_("bvu", [1, H * DH], F32)
            beff = s_("beff", [1, H * DH], F32)
            beff_bf = s_("beffbf", [1, H * DH], BF16)
            biasb_sb = s_("biasb", [128, H * DH], F32)
            t1w = QP.dma(wvdt_sb[:, :], wvdT_d.ap())
            t2w = QP.dma(wvu_sb[:, :], wvu_d.ap())
            t3w = QP.dma(bvdT_sb[:, :], bvdT_d.ap())
            t4w = QS.dma(bvu_sb[:, :], bvu_d.ap())
            tw_q = wload(wq_l, wqb_d, QP)
            tw_k = wload(wk_l, wkb_d, QP)
            ev_toks = []
            for et in range(KT):
                for hp in range(NP):
                    bank = ps_qk[(et * NP + hp) % 2]
                    free = ev_toks[-2] if len(ev_toks) >= 2 else None
                    mt = None
                    for h01 in range(2):
                        h = 2 * hp + h01
                        mt = kb.op("T", lambda eng, e=et, hh=h, o=h01 * 64, b=bank:
                                   eng.matmul(b[:, o:o + 64],
                                              wvdt_sb[0:16,
                                                      hh * D + e * 128:hh * D + (e + 1) * 128],
                                              wvu_sb[0:16, hh * DH:(hh + 1) * DH],
                                              start=True, stop=True),
                                   waits=[t1w, t2w, free] if h01 == 0 else [],
                                   inc=sT if h01 == 1 else None)
                    ev_toks.append(kb.op("V", lambda eng, e=et, p=hp, b=bank:
                                         eng.tensor_copy(
                                             wv_m[:, e * D + p * 128:e * D + (p + 1) * 128],
                                             b[:, 0:128]),
                                         waits=[mt], inc=sV))
            tw_v = ev_toks[-1]
            rowtoks = {}
            for h in range(H):
                bank = ps_v[h // 8]
                rowtoks[h] = kb.op("T", lambda eng, hh=h, b=bank:
                                   eng.matmul(b[0:1, (hh % 8) * 64:(hh % 8 + 1) * 64],
                                              bvdT_sb[0:16, hh:hh + 1],
                                              wvu_sb[0:16, hh * DH:(hh + 1) * DH],
                                              start=True, stop=True),
                                   waits=[t2w, t3w],
                                   inc=sT if h in (7, 15) else None)
            kb.op("V", lambda eng: eng.tensor_tensor(beff[0:1, 0:512], ps_v[0][0:1, :],
                                                     bvu_sb[0:1, 0:512], ALU.add),
                  waits=[rowtoks[7], t4w], inc=sV)
            kb.op("V", lambda eng: eng.tensor_tensor(beff[0:1, 512:1024], ps_v[1][0:1, :],
                                                     bvu_sb[0:1, 512:1024], ALU.add),
                  waits=[rowtoks[15]], inc=sV)
            bt = kb.op("V", lambda eng: eng.tensor_copy(beff_bf[:, :], beff[:, :]), inc=sV)
            h1 = kb.op("T", lambda eng: eng.matmul(ps_r[:, :], ones_row[:, :],
                                                   beff_bf[0:1, 0:512],
                                                   start=True, stop=True),
                       waits=[bt], inc=sT)
            h2 = kb.op("T", lambda eng: eng.matmul(ps_m[:, :], ones_row[:, :],
                                                   beff_bf[0:1, 512:1024],
                                                   start=True, stop=True),
                       inc=sT)
            kb.op("V", lambda eng: eng.tensor_copy(biasb_sb[:, 0:512], ps_r[:, :]),
                  waits=[h1], inc=sV)
            bb2 = kb.op("V", lambda eng: eng.tensor_copy(biasb_sb[:, 512:1024], ps_m[:, :]),
                        waits=[h2], inc=sV)
            st["free_r"] = bb2
            st["free_m"] = bb2

        def ln_stats(c_):
            xtoks = st["xtok"]
            t1 = t2 = None
            buf_tok = {}
            for kt in range(KT):
                cs = kt * C
                par = kt % 2
                sqt = kb.op("V", lambda eng, o=cs, p=par:
                            eng.tensor_mul(sq_buf[:, p * C:(p + 1) * C],
                                           x_bf[:, o:o + C], x_bf[:, o:o + C]),
                            waits=[xtoks[kt], buf_tok.get(par)], inc=sV)
                t1 = kb.op("T", lambda eng, k=kt, o=cs:
                           eng.matmul(ps_sx[0:1, :], ones_col[:, :], x_bf[:, o:o + C],
                                      start=(k == 0), stop=(k == KT - 1)),
                           waits=[xtoks[kt]] + ([st["free_sx"]] if kt == 0 else []),
                           inc=sT)
                t2 = kb.op("T", lambda eng, k=kt, p=par:
                           eng.matmul(ps_sq[0:1, :], ones_col[:, :],
                                      sq_buf[:, p * C:(p + 1) * C],
                                      start=(k == 0), stop=(k == KT - 1)),
                           waits=[sqt] + ([st["free_sq"]] if kt == 0 else []), inc=sT)
                buf_tok[par] = t2
            st["sum_toks"] = (t1, t2)

        def ln_rows():
            t1, t2 = st["sum_toks"]
            m1 = kb.op("V", lambda eng: eng.tensor_scalar_mul(r_mean[:, :],
                                                              ps_sx[0:1, :], 1.0 / D),
                       waits=[t1], inc=sV)
            kb.op("V", lambda eng: eng.tensor_mul(r_var[:, :], r_mean[:, :],
                                                  r_mean[:, :]), inc=sV)
            vt = kb.op("V", lambda eng: eng.scalar_tensor_tensor(
                r_var[:, :], ps_sq[0:1, :], 1.0 / D, r_var[:, :],
                ALU.mult, ALU.subtract), waits=[t2], inc=sV)
            st["free_sx"], st["free_sq"] = m1, vt
            kb.op("A", lambda eng: eng.activation(r_rstd[:, :], r_var[:, :], AF.Ln,
                                                  bias=consts[0:1, EPS_C:EPS_C + 1],
                                                  scale=1.0),
                  waits=[vt], inc=sA)
            at = kb.op("A", lambda eng: eng.activation(r_rstd[:, :], r_rstd[:, :],
                                                       AF.Exp, bias=0.0,
                                                       scale=consts[0:1, NEGH_C:NEGH_C + 1]),
                       inc=sA)
            kb.op("V", lambda eng: eng.tensor_mul(r_mrs[:, :], r_mean[:, :],
                                                  r_rstd[:, :]),
                  waits=[at], inc=sV)
            kb.op("V", lambda eng: eng.tensor_copy(r_rstd_bf[:, :], r_rstd[:, :]),
                  waits=[at], inc=sV)
            st["row_tok"] = kb.op("V", lambda eng: eng.tensor_copy(r_mrs_bf[:, :],
                                                                   r_mrs[:, :]), inc=sV)

        def ln_norm(c_):
            yh = (c_ % 2) * KT * C
            b1 = kb.op("T", lambda eng: eng.matmul(ps_r[:, :], ones_row[:, :],
                                                   r_rstd_bf[:, :],
                                                   start=True, stop=True),
                       waits=[st["row_tok"], st["free_r"]], inc=sT)
            b2 = kb.op("T", lambda eng: eng.matmul(ps_m[:, :], ones_row[:, :],
                                                   r_mrs_bf[:, :],
                                                   start=True, stop=True),
                       waits=[st["free_m"]], inc=sT)
            e1 = kb.op("V", lambda eng: eng.tensor_copy(rb_r[:, :], ps_r[:, :]),
                       waits=[b1, st["rb_used"]], inc=sV)
            e2 = kb.op("V", lambda eng: eng.tensor_copy(rb_m[:, :], ps_m[:, :]),
                       waits=[b2], inc=sV)
            st["free_r"], st["free_m"] = e1, e2
            out_tok = None
            for kt in range(KT):
                cs = kt * C
                kb.op("V", lambda eng, o=cs:
                      eng.tensor_tensor(t1_buf[:, 0:C],
                                        x_bf[:, o:o + C], rb_r[:, :], ALU.mult),
                      waits=[e1, st["y_used"][c_ % 2]], inc=sV)
                out_tok = kb.op("V", lambda eng, o=cs, y0=yh:
                                eng.tensor_tensor(y_sb[:, y0 + o:y0 + o + C],
                                                  t1_buf[:, 0:C],
                                                  rb_m[:, :], ALU.subtract),
                                waits=[e2], inc=sV)
            st["rb_used"] = out_tok
            st["xbf_free"] = out_tok
            st["norm_done"][c_] = out_tok
            return out_tok

        ln_stats(corder[0])
        ln_rows()
        ln_norm(corder[0])

        evac = {"q": [], "k": []}
        tok_vstore = []
        for oi, c_ in enumerate(corder):
            yh = (c_ % 2) * KT * C
            ytok = st["norm_done"][c_]
            if oi + 1 < NCH:
                x_load(corder[oi + 1])
            last_proj = None
            for nm, wsb, wtok, out_sb, sc_col, bias_sb, scr in (
                    ("q", wq_l, tw_q, qch, (QSA_C if variant == "a" else QSB_C),
                     (bqbs if variant == "b" else None), qscr),
                    ("k", wk_l, tw_k, kch, None,
                     (bkb if variant == "b" else None), kscr)):
                evl = evac[nm]
                for jt in range(KT):
                    bank = ps_qk[jt % 2]
                    free = evl[-2] if len(evl) >= 2 else None
                    mt = None
                    for kt in range(KT):
                        mt = kb.op("T", lambda eng, b=bank, w=wsb, j=jt, k=kt, y0=yh:
                                   eng.matmul(b[:, :],
                                              w[:, k * D + j * 128:k * D + (j + 1) * 128],
                                              y_sb[:, y0 + k * C:y0 + (k + 1) * C],
                                              start=(k == 0), stop=(k == KT - 1)),
                                   waits=[ytok, wtok, free] if kt == 0 else [],
                                   inc=sT if kt == KT - 1 else None)
                    sc_ap = None if sc_col is None else consts[:, sc_col:sc_col + 1]

                    def mkfn(b=bank, o=out_sb, j=jt, sc=sc_ap, bs=bias_sb):
                        def fn(eng):
                            return eng.activation(
                                o[:, j * C:(j + 1) * C], b[:, :],
                                AF.Identity if bs is not None else AF.Copy,
                                bias=(bs[:, j:j + 1] if bs is not None else 0.0),
                                scale=(sc if sc is not None else 1.0))
                        return fn
                    evl.append(kb.op("A", mkfn(), waits=[mt], inc=sA))
                    last_proj = mt
                    if nm == "q" and oi + 1 < NCH:
                        if jt == 2:
                            ln_stats(corder[oi + 1])
                        elif jt == 4:
                            ln_rows()
                        elif jt == 7:
                            ln_norm(corder[oi + 1])
                dstq = AP(scr, c_ * C, [[NP * S, 128], [S, KT], [1, C]])
                srcq = AP(out_sb, 0, [[KT * C, 128], [C, KT], [1, C]])
                QS.dma(dstq, srcq, waits=[evl[-1]])

            v_evac = []
            war_v = tok_vstore[-1] if tok_vstore else None
            for ts in range(4):
                for hf in range(2):
                    bank = ps_v[hf]
                    free = v_evac[-2] if len(v_evac) >= 2 else None
                    mt = None
                    for kt in range(KT):
                        mt = kb.op("T", lambda eng, b=bank, t=ts, k=kt, h=hf, y0=yh:
                                   eng.matmul(b[:, :],
                                              y_sb[:, y0 + k * C + t * 128:
                                                   y0 + k * C + (t + 1) * 128],
                                              wv_m[:, k * D + h * 512:k * D + (h + 1) * 512],
                                              start=(k == 0), stop=(k == KT - 1)),
                                   waits=[ytok, tw_v, free] if kt == 0 else [],
                                   inc=sT if kt == KT - 1 else None)
                    off = ts * 130 + hf * 4 * 520
                    dst4 = AP(vch, off, [[NP * 520, 128], [520, 4], [65, 2], [1, 64]])
                    src4 = AP(bank, 0, [[C, 128], [128, 4], [64, 2], [1, 64]])
                    if variant == "a":
                        v_evac.append(kb.op("V", lambda eng, d=dst4, s2=src4:
                                            eng.tensor_copy(d, s2),
                                            waits=[mt, war_v], inc=sV))
                    else:
                        b4 = AP(biasb_sb, hf * 512,
                                [[H * DH, 128], [128, 4], [64, 2], [1, 64]])
                        v_evac.append(kb.op("V", lambda eng, d=dst4, s2=src4, b=b4:
                                            eng.tensor_tensor(d, s2, b, ALU.add),
                                            waits=[mt, war_v], inc=sV))
                    last_proj = mt
            otok = kb.op("P", lambda eng:
                         eng.memset(AP(vch, 64, [[NP * 520, 128], [65, 64]]), 1.0),
                         waits=[war_v], inc=sP)
            dstv = AP(vscr, c_ * 520, [[NP * NT130, 128], [NT130, NP], [1, 520]])
            srcv = AP(vch, 0, [[NP * 520, 128], [520, NP], [1, 520]])
            tok_vstore.append(QS.dma(dstv, srcv, waits=[v_evac[-1], otok]))
            st["y_used"][c_ % 2] = last_proj

        kb.emit_block(f"prep_{variant}")
        seg.close()

    def emit_attend(variant):
        seg = ExitStack()
        pfx = f"at{variant}"
        s_ = lambda name, shape, dt: seg.enter_context(
            nc.sbuf_tensor(f"{pfx}_{name}", shape, dt))
        ps_ = lambda name, shape: seg.enter_context(
            nc.psum_tensor(f"{pfx}_ps_{name}", shape, F32))
        kbuf = [s_(f"k{i}", [128, S], BF16) for i in range(5)]
        vbuf = [s_(f"v{i}", [128, NT130], BF16) for i in range(5)]
        qbuf = s_("q", [128, 2 * NP * C], BF16)
        x_sb = s_("x", [128, 2 * KT * C], F32)
        attn_sb = s_("attn", [128, 2 * NP * C], BF16)
        p_sb = s_("p", [128, 3 * 2 * C], BF16)
        rb_sb = s_("rb", [128, C], BF16)
        stage = s_("stage", [128, 2 * C], F32)
        mask_sb = s_("mask", [128, 4 * 2 * C], BF16)
        lnr = s_("lnr", [33, C], F32)
        rd_sb = s_("rd", [33, C], BF16)
        wo_l = s_("wo", [128, KT * D], BF16) if variant == "a" else None
        ps_s = [ps_("s0", [128, 2 * C]), ps_("s1", [128, 2 * C])]
        ps_pv = [ps_("pv0", [128, C]), ps_("pv1", [128, C])]
        ps_db = [ps_("db0", [128, C]), ps_("db1", [128, C])]

        x_in = xT if variant == "a" else xres
        kb.op("P", lambda eng: eng.memset(rd_sb[:, :], 0.0), inc=sP)
        orow = s_("orow", [1, C], BF16)
        ofill = kb.op("P", lambda eng: eng.memset(orow[:, :], 1.0), inc=sP)
        fill0 = kb.op("T", lambda eng: eng.matmul(ps_db[0][:, :], ones_row[:, :],
                                                  orow[:, :], start=True, stop=True),
                      waits=[ofill], inc=sT)
        fill1 = kb.op("T", lambda eng: eng.matmul(ps_db[1][:, :], ones_row[:, :],
                                                  orow[:, :], start=True, stop=True),
                      inc=sT)
        db_fill = [fill0, fill1]

        dq_slices = []
        if variant == "a":
            dqf = s_("dqf", [128, 2048], BF16)
            dqs = s_("dqs", [128, 2048], BF16)
            dqw = s_("dqw", [128, 2048], BF16)
            dq_state = {"stt": None, "out": None, "in": None}

            def mk_dq(qsrc, ssrc, wscr, kt, col0, width, qw):
                def emit_dma():
                    tq = QP.dma(AP(dqf, 0, [[2048, 128], [1, width]]),
                                AP(qsrc, kt * 128 * qw + col0, [[qw, 128], [1, width]]),
                                waits=[dq_state["stt"]])
                    ts_ = QP.dma(AP(dqs, 0, [[2048, 128], [1, width]]),
                                 AP(ssrc, kt * qw + col0, [[0, 128], [1, width]]),
                                 waits=[dq_state["stt"]])
                    dq_state["in"] = (tq, ts_)

                def emit_compute():
                    w_ = dq_state["out"]
                    tq, ts_ = dq_state["in"]
                    h2 = width // 2
                    kb.op("V", lambda eng, w2=h2:
                          eng.scalar_tensor_tensor(dqw[:, 0:w2], dqf[:, 0:w2],
                                                   128.0, dqs[:, 0:w2],
                                                   ALU.subtract, ALU.mult),
                          waits=[tq, ts_, w_], inc=sV)
                    stt = kb.op("V", lambda eng, o=h2, w2=h2:
                                eng.scalar_tensor_tensor(dqw[:, o:o + w2],
                                                         dqf[:, o:o + w2],
                                                         128.0, dqs[:, o:o + w2],
                                                         ALU.subtract, ALU.mult),
                                inc=sV)
                    dq_state["stt"] = stt
                    dq_state["out"] = QP.dma(
                        AP(wscr, kt * qw + col0, [[KT * FF, 128], [1, width]]),
                        AP(dqw, 0, [[2048, 128], [1, width]]), waits=[stt])
                return emit_dma, emit_compute
            for kt in range(KT):
                for hf2 in range(2):
                    dq_slices.append(mk_dq(fc1q, s1rows, w1scr, kt, hf2 * 2048,
                                           2048, FF))
            for kt in range(32):
                dq_slices.append(mk_dq(fc2q, s2rows, w2scr, kt, 0, 1024, D))

        def kv_range(c_):
            return (0, 4 * c_ + 4) if variant == "a" else (4 * c_, 32)

        tok_kv = {}
        kv_free = {}

        def load_kv(c_, hp, buf, prev_key):
            j0, j1 = kv_range(c_)
            n = j1 - j0
            nh = (n + 1) // 2
            wk_, wv_ = kv_free.get(prev_key, (None, None))
            tk1 = QP.dma(AP(kbuf[buf], 0, [[S, 128], [1, nh * 128]]),
                         AP(kscr, hp * S + j0 * 128, [[NP * S, 128], [1, nh * 128]]),
                         waits=[wk_])
            tv1 = QP.dma(AP(vbuf[buf], 0, [[NT130, 128], [1, nh * 130]]),
                         AP(vscr, hp * NT130 + j0 * 130,
                            [[NP * NT130, 128], [1, nh * 130]]),
                         waits=[wv_])
            if n > nh:
                tk2 = QP.dma(AP(kbuf[buf], nh * 128, [[S, 128], [1, (n - nh) * 128]]),
                             AP(kscr, hp * S + (j0 + nh) * 128,
                                [[NP * S, 128], [1, (n - nh) * 128]]))
                tv2 = QP.dma(AP(vbuf[buf], nh * 130,
                                [[NT130, 128], [1, (n - nh) * 130]]),
                             AP(vscr, hp * NT130 + (j0 + nh) * 130,
                                [[NP * NT130, 128], [1, (n - nh) * 130]]))
            else:
                tk2, tv2 = tk1, tv1
            tok_kv[(c_, hp)] = (tk1, tk2, tv1, tv2, nh, buf)

        exp_free = {0: None, 1: None}
        p_free = {0: None, 1: None, 2: None}
        norm_free = {0: None, 1: None}
        state = {"rd_free": None, "rb_free": None, "db_free": [None, None],
                 "tok_x_free": [None, None], "tok_x": [None, None],
                 "wo_bank_free": None, "wo_norm_toks": [None, None]}
        attn_free = {0: [None] * NP, 1: [None] * NP}
        tok_q = [None, None]
        tok_q_last = {}
        stage_hist = []
        deferred = []
        def_wo = []
        chunk_order = (list(range(NCH)) if variant == "a"
                       else list(range(NCH - 1, -1, -1)))
        cpos = {c2: i2 for i2, c2 in enumerate(chunk_order)}
        pairs = [(c_, hp) for c_ in chunk_order for hp in range(NP)]
        for ii in range(5):
            load_kv(*pairs[ii], ii, None)

        def q_load(c_):
            half = c_ % 2
            oi = cpos[c_]
            return QS.dma(
                AP(qbuf, half * NP * C, [[2 * NP * C, 128], [C, NP], [1, C]]),
                AP(qscr, c_ * C, [[NP * S, 128], [S, NP], [1, C]]),
                waits=[tok_q_last.get(chunk_order[oi - 2]) if oi >= 2 else None])

        tok_q[chunk_order[0] % 2] = q_load(chunk_order[0])
        tok_mask = QS.dma(mask_sb[:, :], (maska_d if variant == "a" else maskb_d).ap())
        tok_wo = wload(wo_l, woaT, QP) if variant == "a" else None

        def emit_qk(c_, hp, i, tk, buf):
            sp = i % 2
            sbank = ps_s[sp]
            qo = (c_ % 2) * NP * C + hp * C
            kb.op("T", lambda eng, b=sbank, kk=kbuf[buf], ii=i, q0=qo:
                  eng.matmul(b[:, 0:C], kk[0:64, ii * 128:(ii + 1) * 128],
                             qbuf[0:64, q0:q0 + C], start=True, stop=True,
                             tile_position=(0, 0)),
                  waits=[tk, tok_q[c_ % 2], exp_free[sp]])
            return kb.op("T", lambda eng, b=sbank, kk=kbuf[buf], ii=i, q0=qo:
                         eng.matmul(b[:, C:2 * C],
                                    kk[64:128, ii * 128:(ii + 1) * 128],
                                    qbuf[64:128, q0:q0 + C],
                                    start=True, stop=True,
                                    tile_position=(64, 0)),
                         inc=sT)

        def emit_tail(c_, hp, par, xh, pvb, last_pvden):
            dbb = ps_db[par]
            kb.op("A", lambda eng, b=dbb:
                  eng.activation(lnr[0:33, :], b[0:33, :], AF.Ln),
                  waits=[last_pvden, state["rd_free"]], inc=sA)
            recip_tok = kb.op("A", lambda eng:
                              eng.activation(rd_sb[0:33, :], lnr[0:33, :], AF.Exp,
                                             bias=0.0,
                                             scale=consts[0:33, NEG1_C:NEG1_C + 1]),
                              inc=sA)
            tb = kb.op("T", lambda eng, b=dbb:
                       eng.matmul(b[:, :], ematD[:, :], rd_sb[:, :],
                                  start=True, stop=True),
                       waits=[recip_tok], inc=sT)
            state["rd_free"] = tb
            rbt = kb.op("V", lambda eng, b=dbb: eng.tensor_copy(rb_sb[:, :], b[:, :]),
                        waits=[tb, state["rb_free"]], inc=sV)
            state["db_free"][par] = rbt
            ah = xh * NP * C
            nt_ = kb.op("V", lambda eng, p=hp, b=pvb, a0=ah:
                        eng.tensor_tensor(attn_sb[:, a0 + p * C:a0 + (p + 1) * C],
                                          b[:, :], rb_sb[:, :], ALU.mult),
                        waits=[rbt, attn_free[xh][hp]], inc=sV)
            state["rb_free"] = nt_
            norm_free[par] = nt_
            if variant == "b":
                war = stage_hist[-2][1] if len(stage_hist) >= 2 else None
                st_ = kb.op("V", lambda eng, p=hp, sp2=par, x0=xh * KT * C, a0=ah:
                            eng.tensor_tensor(stage[:, sp2 * C:(sp2 + 1) * C],
                                              attn_sb[:, a0 + p * C:a0 + (p + 1) * C],
                                              x_sb[:, x0 + p * C:x0 + (p + 1) * C],
                                              ALU.add),
                            waits=[nt_, state["tok_x"][xh][hp], war], inc=sV)
                attn_free[xh][hp] = st_
                dmat = QO.dma(AP(xres, hp * 128 * S + c_ * C, [[S, 128], [1, C]]),
                              stage[:, par * C:(par + 1) * C], waits=[st_])
                stage_hist.append((st_, dmat))
                if hp == NP - 1:
                    state["tok_x_free"][xh] = st_

        def emit_wo_group(c_, it, xh):
            bank = ps_pv[1]
            norm_toks = [state["wo_norm_toks"][0], state["wo_norm_toks"][1],
                         norm_free[1]]
            ah = xh * NP * C
            mt = None
            for jt in range(KT):
                mt = kb.op("T", lambda eng, b=bank, i2=it, j2=jt, a0=ah:
                           eng.matmul(b[:, :],
                                      wo_l[:, j2 * D + i2 * 128:
                                           j2 * D + (i2 + 1) * 128],
                                      attn_sb[:, a0 + j2 * C:a0 + (j2 + 1) * C],
                                      start=(j2 == 0), stop=(j2 == KT - 1)),
                           waits=([tok_wo, state["wo_bank_free"]] + norm_toks)
                           if jt == 0 else [],
                           inc=sT if jt == KT - 1 else None)
            war = stage_hist[-2][1] if len(stage_hist) >= 2 else None
            st_ = kb.op("V", lambda eng, b=bank, i2=it, sp2=it % 2, x0=xh * KT * C:
                        eng.tensor_tensor(stage[:, sp2 * C:(sp2 + 1) * C],
                                          b[:, :],
                                          x_sb[:, x0 + i2 * C:x0 + (i2 + 1) * C],
                                          ALU.add),
                        waits=[mt, state["tok_x"][xh][it], war], inc=sV)
            dmat = QO.dma(AP(xres, it * 128 * S + c_ * C, [[S, 128], [1, C]]),
                          stage[:, (it % 2) * C:(it % 2 + 1) * C], waits=[st_])
            stage_hist.append((st_, dmat))
            state["wo_bank_free"] = st_
            if it == KT - 1:
                for p2 in range(NP):
                    attn_free[xh][p2] = mt
                state["tok_x_free"][xh] = st_

        pending_qk = None
        ucnt = [0]
        for idx, (c_, hp) in enumerate(pairs):
            par = hp % 2
            xh = c_ % 2
            j0, j1 = kv_range(c_)
            nkt = j1 - j0
            if hp == 0:
                oi = cpos[c_]
                if oi + 1 < NCH:
                    tok_q[chunk_order[oi + 1] % 2] = q_load(chunk_order[oi + 1])
                xw = state["tok_x_free"][xh]
                state["tok_x"][xh] = [
                    QS.dma(AP(x_sb, xh * KT * C + p2 * C, [[2 * KT * C, 128], [1, C]]),
                           AP(x_in, p2 * 128 * S + c_ * C, [[S, 128], [1, C]]),
                           waits=[xw])
                    for p2 in range(KT)]
            tk1, tk2, tv1, tv2, nh_, buf = tok_kv.pop((c_, hp))
            dq_pending = None
            if dq_slices and idx >= 16:
                dq_pending = dq_slices.pop(0)
                dq_pending[0]()
            pvb = ps_pv[par]
            qk_tok = {}
            if pending_qk is not None:
                qk_tok[0] = pending_qk
                pending_qk = None
            else:
                qk_tok[0] = emit_qk(c_, hp, 0, tk1, buf)
            last_pvden = None
            for i in range(nkt):
                j = j0 + i
                sp = i % 2
                pp = ucnt[0] % 3
                ucnt[0] += 1
                d = j - 4 * c_
                if 0 <= d < 4:
                    te0 = kb.op("A", lambda eng, b=ps_s[sp], sp2=pp:
                                eng.activation(p_sb[:, sp2 * 2 * C:sp2 * 2 * C + C],
                                               b[:, 0:C], AF.Exp),
                                waits=[qk_tok[i], p_free[pp]], inc=sA)
                    m0 = kb.op("V", lambda eng, sp2=pp, dd=d:
                               eng.tensor_mul(
                                   p_sb[:, sp2 * 2 * C:sp2 * 2 * C + C],
                                   p_sb[:, sp2 * 2 * C:sp2 * 2 * C + C],
                                   mask_sb[:, dd * 2 * C:dd * 2 * C + C]),
                               waits=[te0, tok_mask], inc=sV)
                    te1 = kb.op("A", lambda eng, b=ps_s[sp], sp2=pp:
                                eng.activation(
                                    p_sb[:, sp2 * 2 * C + C:sp2 * 2 * C + 2 * C],
                                    b[:, C:2 * C], AF.Exp),
                                inc=sA)
                    m1 = kb.op("V", lambda eng, sp2=pp, dd=d:
                               eng.tensor_mul(
                                   p_sb[:, sp2 * 2 * C + C:sp2 * 2 * C + 2 * C],
                                   p_sb[:, sp2 * 2 * C + C:sp2 * 2 * C + 2 * C],
                                   mask_sb[:, dd * 2 * C + C:(dd + 1) * 2 * C]),
                               waits=[te1], inc=sV)
                    exp_free[sp] = te1
                    pv_wait, pv_wait_o = m0, m1
                else:
                    te = kb.op("A", lambda eng, b=ps_s[sp], sp2=pp:
                               eng.activation(p_sb[:, sp2 * 2 * C:(sp2 + 1) * 2 * C],
                                              b[:, :], AF.Exp),
                               waits=[qk_tok[i], p_free[pp]], inc=sA)
                    exp_free[sp] = te
                    pv_wait = pv_wait_o = te
                if i + 1 < nkt:
                    qk_tok[i + 1] = emit_qk(c_, hp, i + 1,
                                            tk1 if i + 1 < nh_ else tk2, buf)
                elif idx + 1 < len(pairs):
                    nc3, nh3 = pairs[idx + 1]
                    ntk1, _, _, _, _, nbuf = tok_kv[(nc3, nh3)]
                    pending_qk = emit_qk(nc3, nh3, 0, ntk1, nbuf)
                w1 = [pv_wait, tv1 if i < nh_ else tv2]
                if i == 0:
                    w1 += [norm_free[par], state["db_free"][par]]
                    if par == 1 and variant == "a":
                        w1.append(state["wo_bank_free"])
                i0, il = (i == 0), (i == nkt - 1)
                kb.op("T", lambda eng, b=pvb, vv=vbuf[buf], ii=i, sp2=pp, a=i0, z=il:
                      eng.matmul(b[0:64, :], vv[:, ii * 130:ii * 130 + 64],
                                 p_sb[:, sp2 * 2 * C:sp2 * 2 * C + C],
                                 start=a, stop=z, tile_position=(0, 0)),
                      waits=w1)
                kb.op("T", lambda eng, b=pvb, vv=vbuf[buf], ii=i, sp2=pp, a=i0, z=il:
                      eng.matmul(b[64:128, :], vv[:, ii * 130 + 65:ii * 130 + 129],
                                 p_sb[:, sp2 * 2 * C + C:sp2 * 2 * C + 2 * C],
                                 start=a, stop=z, tile_position=(0, 64)),
                      waits=[pv_wait_o])
                kb.op("T", lambda eng, b=ps_db[par], vv=vbuf[buf], ii=i, sp2=pp, a=i0,
                      z=il:
                      eng.matmul(b[0:1, :], vv[:, ii * 130 + 64:ii * 130 + 65],
                                 p_sb[:, sp2 * 2 * C:sp2 * 2 * C + C],
                                 start=a, stop=z, tile_position=(0, 0)),
                      waits=[db_fill[par]] if i == 0 else [])
                last_pvden = kb.op("T", lambda eng, b=ps_db[par], vv=vbuf[buf], ii=i,
                                   sp2=pp, a=i0, z=il:
                                   eng.matmul(b[32:33, :],
                                              vv[:, ii * 130 + 129:ii * 130 + 130],
                                              p_sb[:, sp2 * 2 * C + C:
                                                   sp2 * 2 * C + 2 * C],
                                              start=a, stop=z,
                                              tile_position=(0, 32)),
                                   inc=sT)
                p_free[pp] = last_pvden
                if i in (3, 4) and deferred:
                    deferred.pop(0)()
                elif (i >= 5 and i % 2 == 1 and par == 0 and not deferred
                      and def_wo):
                    def_wo.pop(0)()
            while deferred:
                deferred.pop(0)()
            kv_free[(c_, hp)] = (qk_tok[nkt - 1], last_pvden)
            if hp == NP - 1:
                tok_q_last[c_] = qk_tok[nkt - 1]
            if idx + 5 < len(pairs):
                load_kv(*pairs[idx + 5], buf, (c_, hp))
            deferred.append(lambda c2=c_, h2=hp, p2=par, x2=xh, b2=pvb,
                            r2=last_pvden: emit_tail(c2, h2, p2, x2, b2, r2))
            if dq_pending is not None:
                dq_pending[1]()
            if variant == "a" and hp == NP - 1:
                if def_wo:
                    while deferred:
                        deferred.pop(0)()
                    while def_wo:
                        def_wo.pop(0)()
                deferred.append(lambda: state.__setitem__(
                    "wo_norm_toks", [norm_free[0], norm_free[1]]))
                for it in range(KT):
                    def_wo.append(lambda c2=c_, i2=it, x2=xh:
                                  emit_wo_group(c2, i2, x2))
            if idx == len(pairs) - 1:
                while deferred:
                    deferred.pop(0)()
                while def_wo:
                    def_wo.pop(0)()
        kb.emit_block(f"attend_{variant}")
        seg.close()

    def emit_ffn(first_call):
        out_dram = xres if first_call else outT
        seg = ExitStack()
        pfx = "f1" if first_call else "f2"
        s_ = lambda name, shape, dt: seg.enter_context(
            nc.sbuf_tensor(f"{pfx}_{name}", shape, dt))
        ps_ = lambda name, shape: seg.enter_context(
            nc.psum_tensor(f"{pfx}_ps_{name}", shape, F32))
        w1_sb = s_("w1", [128, KT * FF], BF16)
        w2_sb = s_("w2", [128, 32 * D], BF16)
        ps_sx = ps_("sx", [1, C])
        ps_sq = ps_("sq", [1, C])
        ps_r = ps_("r", [128, C])
        ps_m = ps_("m", [128, C])
        ps_f1 = [ps_("f1a", [128, HC]), ps_("f1b", [128, HC])]
        ps_f2 = [ps_("f2a", [128, HC]), ps_("f2b", [128, HC])]

        w1_tok = [None] * KT
        w2_tok = [None] * 32

        x_sb = s_("x", [128, KT * C], F32)
        x_bf = s_("xbf", [128, KT * C], BF16)
        y_sb = s_("y", [128, KT * C], BF16)
        h_sb = s_("h", [128, KT * C], BF16)
        a_sb = s_("a", [128, 32 * HC], BF16)
        t2f_buf = s_("t2f", [128, KT * C], BF16)
        sq_buf = s_("sqb", [128, 2 * C], BF16)
        rb_r = s_("rbr", [128, C], BF16)
        rb_m = s_("rbm", [128, C], BF16)
        t1_buf = s_("t1", [128, C], BF16)
        z_sb = s_("z", [128, 2 * HC], F32)
        stage = s_("stage", [128, 2 * HC], F32)
        st = {"free_sx": None, "free_sq": None, "free_r": None, "free_m": None,
              "xbf_free": None, "y_used": None, "sum_toks": None, "row_tok": None,
              "xbf_tok": None, "h_used": None, "x_free": None, "x_tok": None,
              "xy_done": None, "h_aff": {}, "ivt": None, "rb_used": None}

        def xbf_load(c_, sliced=False):
            if sliced:
                st["xbf_tok"] = [
                    QP.dma(AP(x_bf, k2 * C, [[KT * C, 128], [1, C]]),
                           AP(xres, k2 * 128 * S + c_ * C, [[S, 128], [1, C]]),
                           waits=[st["xbf_free"]])
                    for k2 in range(KT)]
            else:
                t_ = QP.dma(
                    AP(x_bf, 0, [[KT * C, 128], [C, KT], [1, C]]),
                    AP(xres, c_ * C, [[S, 128], [128 * S, KT], [1, C]]),
                    waits=[st["xbf_free"]])
                st["xbf_tok"] = [t_] * KT

        def x_load(c_):
            st["x_tok"] = QS.dma(
                AP(x_sb, 0, [[KT * C, 128], [C, KT], [1, C]]),
                AP(xres, c_ * C, [[S, 128], [128 * S, KT], [1, C]]),
                waits=[st["x_free"]])

        def ln_stats(src_sb, src_tok, extra=None):
            tks = src_tok if isinstance(src_tok, list) else [src_tok] * KT
            t1 = t2 = None
            buf_tok = {}
            for kt in range(KT):
                cs = kt * C
                par = kt % 2
                sqt = kb.op("V", lambda eng, o=cs, p=par, sb2=src_sb:
                            eng.tensor_mul(sq_buf[:, p * C:(p + 1) * C],
                                           sb2[:, o:o + C], sb2[:, o:o + C]),
                            waits=[tks[kt], buf_tok.get(par)], inc=sV)
                t1 = kb.op("T", lambda eng, k=kt, o=cs, sb2=src_sb:
                           eng.matmul(ps_sx[0:1, :], ones_col[:, :], sb2[:, o:o + C],
                                      start=(k == 0), stop=(k == KT - 1)),
                           waits=[tks[kt]] + ([st["free_sx"]] if kt == 0 else []),
                           inc=sT)
                t2 = kb.op("T", lambda eng, k=kt, p=par:
                           eng.matmul(ps_sq[0:1, :], ones_col[:, :],
                                      sq_buf[:, p * C:(p + 1) * C],
                                      start=(k == 0), stop=(k == KT - 1)),
                           waits=[sqt] + ([st["free_sq"]] if kt == 0 else []), inc=sT)
                buf_tok[par] = t2
            st["sum_toks"] = (t1, t2)

        def ln_rows():
            t1, t2 = st["sum_toks"]
            m1 = kb.op("V", lambda eng: eng.tensor_scalar_mul(r_mean[:, :],
                                                              ps_sx[0:1, :], 1.0 / D),
                       waits=[t1], inc=sV)
            kb.op("V", lambda eng: eng.tensor_mul(r_var[:, :], r_mean[:, :],
                                                  r_mean[:, :]), inc=sV)
            vt = kb.op("V", lambda eng: eng.scalar_tensor_tensor(
                r_var[:, :], ps_sq[0:1, :], 1.0 / D, r_var[:, :],
                ALU.mult, ALU.subtract), waits=[t2], inc=sV)
            st["free_sx"], st["free_sq"] = m1, vt
            kb.op("A", lambda eng: eng.activation(r_rstd[:, :], r_var[:, :], AF.Ln,
                                                  bias=consts[0:1, EPS_C:EPS_C + 1],
                                                  scale=1.0),
                  waits=[vt], inc=sA)
            at = kb.op("A", lambda eng: eng.activation(r_rstd[:, :], r_rstd[:, :],
                                                       AF.Exp, bias=0.0,
                                                       scale=consts[0:1, NEGH_C:NEGH_C + 1]),
                       inc=sA)
            kb.op("V", lambda eng: eng.tensor_mul(r_mrs[:, :], r_mean[:, :],
                                                  r_rstd[:, :]),
                  waits=[at], inc=sV)
            kb.op("V", lambda eng: eng.tensor_copy(r_rstd_bf[:, :], r_rstd[:, :]),
                  waits=[at], inc=sV)
            st["row_tok"] = kb.op("V", lambda eng: eng.tensor_copy(r_mrs_bf[:, :],
                                                                   r_mrs[:, :]), inc=sV)

        def ln_bcast():
            b1 = kb.op("T", lambda eng: eng.matmul(ps_r[:, :], ones_row[:, :],
                                                   r_rstd_bf[:, :],
                                                   start=True, stop=True),
                       waits=[st["row_tok"], st["free_r"]], inc=sT)
            b2 = kb.op("T", lambda eng: eng.matmul(ps_m[:, :], ones_row[:, :],
                                                   r_mrs_bf[:, :],
                                                   start=True, stop=True),
                       waits=[st["free_m"]], inc=sT)
            return b1, b2

        def oln_norm(bc, src_tok, gate):
            tks = src_tok if isinstance(src_tok, list) else [src_tok] * KT
            b1, b2 = bc
            e1 = kb.op("V", lambda eng: eng.tensor_copy(rb_r[:, :], ps_r[:, :]),
                       waits=[b1, st["rb_used"]], inc=sV)
            e2 = kb.op("V", lambda eng: eng.tensor_copy(rb_m[:, :], ps_m[:, :]),
                       waits=[b2], inc=sV)
            st["free_r"], st["free_m"] = e1, e2
            out_tok = None
            for kt in range(KT):
                cs = kt * C
                kb.op("V", lambda eng, o=cs, k=kt:
                      eng.tensor_tensor(t1_buf[:, 0:C],
                                        x_bf[:, o:o + C], rb_r[:, :], ALU.mult),
                      waits=[e1, tks[kt], gate], inc=sV)
                out_tok = kb.op("V", lambda eng, o=cs:
                                eng.tensor_tensor(y_sb[:, o:o + C],
                                                  t1_buf[:, 0:C],
                                                  rb_m[:, :], ALU.subtract),
                                waits=[e2], inc=sV)
            st["rb_used"] = out_tok
            st["xbf_free"] = out_tok
            return out_tok

        def iln_norm_v(bc, ytok):
            b1, b2 = bc
            e1 = kb.op("V", lambda eng: eng.tensor_copy(rb_r[:, :], ps_r[:, :]),
                       waits=[b1, st["rb_used"]], inc=sV)
            e2 = kb.op("V", lambda eng: eng.tensor_copy(rb_m[:, :], ps_m[:, :]),
                       waits=[b2], inc=sV)
            st["free_r"], st["free_m"] = e1, e2
            toks = []
            for kt in range(KT):
                cs = kt * C
                kb.op("V", lambda eng, o=cs:
                      eng.tensor_tensor(t1_buf[:, 0:C],
                                        y_sb[:, o:o + C], rb_r[:, :], ALU.mult),
                      waits=[e1, ytok], inc=sV)
                toks.append(kb.op("V", lambda eng, o=cs:
                                  eng.tensor_tensor(t2f_buf[:, o:o + C],
                                                    t1_buf[:, 0:C],
                                                    rb_m[:, :], ALU.subtract),
                                  waits=[e2, st["h_aff"].get(kt)], inc=sV))
            st["rb_used"] = toks[-1]
            return toks

        def iln_affine(kt, vtoks, gate):
            cs = kt * C
            tok = kb.op("A", lambda eng, o=cs, k=kt:
                        eng.activation(h_sb[:, o:o + C], t2f_buf[:, o:o + C],
                                       AF.Identity, bias=bet[:, k:k + 1],
                                       scale=gam[:, k:k + 1]),
                        waits=[vtoks[kt], gate], inc=sA)
            st["h_aff"][kt] = tok
            return tok

        forder = list(range(NCH - 1, -1, -1)) if first_call else list(range(NCH))
        xbf_load(forder[0], sliced=True)

        Q1 = KT * FF // 4
        w1t = []
        for qi in range(4):
            w1t.append(QP.dma(AP(w1_sb, qi * Q1, [[KT * FF, 128], [1, Q1]]),
                              AP(w1scr, qi * Q1, [[KT * FF, 128], [1, Q1]])))
        H2 = 32 * D // 2
        t2a = QP.dma(AP(w2_sb, 0, [[32 * D, 128], [1, H2]]),
                     AP(w2scr, 0, [[32 * D, 128], [1, H2]]))
        t2b = QP.dma(AP(w2_sb, H2, [[32 * D, 128], [1, H2]]),
                     AP(w2scr, H2, [[32 * D, 128], [1, H2]]))
        for i in range(KT):
            w1_tok[i] = w1t[i // 2]
        for i in range(32):
            w2_tok[i] = t2a if i < 16 else t2b

        ln_stats(x_bf, st["xbf_tok"])
        ln_rows()
        ytok = oln_norm(ln_bcast(), st["xbf_tok"], None)
        xbf_load(forder[1])
        ln_stats(y_sb, ytok)
        ln_rows()
        _ivt = iln_norm_v(ln_bcast(), ytok)
        htok = None
        for _k in range(KT):
            htok = iln_affine(_k, _ivt, None)

        a_free = {}
        silu_toks = {}
        z_free = {}
        stage_dma = []
        for foi, c_ in enumerate(forder):
            x_load(c_)
            my_y = ytok
            xy_tok = None
            my_h = htok
            last_stage = None
            nx = foi + 1 < NCH
            for hf in range(2):
                for nt in range(32):
                    bank = ps_f1[nt % 2]
                    free = silu_toks.get((1 - hf, 30 + nt)) if nt < 2 else                         silu_toks.get((hf, nt - 2))
                    mt = None
                    for kt in range(KT):
                        mt = kb.op("T", lambda eng, b=bank, n=nt, k=kt, h=hf:
                                   eng.matmul(b[:, :],
                                              w1_sb[:, k * FF + n * 128:
                                                    k * FF + (n + 1) * 128],
                                              h_sb[:, k * C + h * HC:k * C + h * HC + HC],
                                              start=(k == 0), stop=(k == KT - 1)),
                                   waits=([my_h, w1_tok[kt], free, a_free.get(nt)]
                                          if kt == 0 else [w1_tok[kt]]),
                                   inc=sT if kt == KT - 1 else None)
                    silu_toks[(hf, nt)] = kb.op(
                        "A", lambda eng, b=bank, n=nt:
                        eng.activation(a_sb[:, n * HC:(n + 1) * HC], b[:, :], AF.Silu,
                                       bias=fc1b[:, n:n + 1], scale=1.0),
                        waits=[mt], inc=sA)
                    if hf == 0:
                        if nx and nt == 4:
                            ln_stats(x_bf, st["xbf_tok"])
                        elif nx and nt == 12:
                            ln_rows()
                        elif nt == 22:
                            for kt2 in range(KT):
                                xy_tok = kb.op(
                                    "V", lambda eng, o=kt2 * C:
                                    eng.tensor_tensor(x_sb[:, o:o + C],
                                                      x_sb[:, o:o + C],
                                                      y_sb[:, o:o + C], ALU.add),
                                    waits=[st["x_tok"], my_y,
                                           st["xy_done"] if kt2 == 0 else None],
                                    inc=sV)
                            st["y_used"] = xy_tok
                        elif nx and nt == 28:
                            ytok = oln_norm(ln_bcast(), st["xbf_tok"], st["y_used"])
                    if nx and hf == 1 and nt == 16:
                        st["ibc"] = ln_bcast()
                if hf == 1 and nx:
                    st["ivt"] = iln_norm_v(st["ibc"], ytok)
                    fc1_last = mt
                for jt in range(KT):
                    bank = ps_f2[jt % 2]
                    zfree = z_free.get(jt % 2)
                    mt = None
                    for nt in range(32):
                        mt = kb.op("T", lambda eng, b=bank, n=nt, j=jt:
                                   eng.matmul(b[:, :],
                                              w2_sb[:, n * D + j * 128:
                                                    n * D + (j + 1) * 128],
                                              a_sb[:, n * HC:(n + 1) * HC],
                                              start=(n == 0), stop=(n == 31)),
                                   waits=[silu_toks[(hf, nt)], w2_tok[nt]] +
                                   ([zfree] if nt == 0 else []),
                                   inc=sT if nt == 31 else None)
                        if jt == KT - 1:
                            a_free[nt] = mt
                    zt = kb.op("A", lambda eng, b=bank, p=jt % 2, j=jt:
                               eng.activation(z_sb[:, p * HC:(p + 1) * HC], b[:, :],
                                              AF.Identity, bias=fc2bh[:, j:j + 1],
                                              scale=consts[:, HALF_C:HALF_C + 1]),
                               waits=[mt, z_free.get(("t", jt % 2))], inc=sA)
                    z_free[jt % 2] = zt
                    war = stage_dma[-2] if len(stage_dma) >= 2 else None
                    st_ = kb.op("V", lambda eng, p=jt % 2, j=jt, h=hf:
                                eng.tensor_tensor(stage[:, p * HC:(p + 1) * HC],
                                                  z_sb[:, p * HC:(p + 1) * HC],
                                                  x_sb[:, j * C + h * HC:
                                                       j * C + h * HC + HC],
                                                  ALU.add),
                                waits=[zt, xy_tok, war], inc=sV)
                    z_free[("t", jt % 2)] = st_
                    stage_dma.append(QO.dma(
                        AP(out_dram, jt * 128 * S + c_ * C + hf * HC,
                           [[S, 128], [1, HC]]),
                        stage[:, (jt % 2) * HC:(jt % 2 + 1) * HC], waits=[st_]))
                    last_stage = st_
                    if nx and hf == 0 and jt == 3:
                        ln_stats(y_sb, ytok)
                    elif nx and hf == 0 and jt == 6:
                        ln_rows()
                    if nx and hf == 1:
                        htok = iln_affine(jt, st["ivt"], fc1_last)
            if nx and foi + 2 < NCH:
                xbf_load(forder[foi + 2])
            st["x_free"] = last_stage
            st["xy_done"] = last_stage
        kb.emit_block("ffn1" if first_call else "ffn2")
        seg.close()

    def emit_debug_copy():
        for kt in range(KT):
            QS.dma(AP(outT, kt * 128 * S, [[S, 128], [1, S]]),
                   AP(xres, kt * 128 * S, [[S, 128], [1, S]]))
        kb.emit_block("dbgcopy")

    emit_p0()
    if PH == 16:
        for kt in range(KT):
            QS.dma(AP(xres, kt * 128 * S, [[S, 128], [1, S]]),
                   AP(xT, kt * 128 * S, [[S, 128], [1, S]]))
        kb.emit_block("dbgseed")
        emit_prep("b")
        emit_attend("b")
        emit_debug_copy()
        stack.close()
        return nc
    if PH >= 1:
        emit_prep("a")
    if PH >= 2:
        emit_attend("a")
    if PH >= 4:
        emit_ffn(True)
    if PH >= 5:
        emit_prep("b")
    if PH >= 6:
        emit_attend("b")
    if PH >= 7:
        emit_ffn(False)
    else:
        emit_debug_copy()
    stack.close()
    return nc


def _prep_inputs(inputs):
    bf = ml_dtypes.bfloat16
    g = {}
    g["wqaT"] = np.ascontiguousarray(inputs["L0_attn_a_Wq"].T)
    g["wkaT"] = np.ascontiguousarray(inputs["L0_attn_a_Wk"].T)
    g["wvaT"] = np.ascontiguousarray(inputs["L0_attn_a_Wv"].T)
    g["woaT"] = np.ascontiguousarray(inputs["L0_attn_a_Wo"].T)
    g["wqb"] = np.ascontiguousarray(
        inputs["L0_attn_b_Wq"].transpose(1, 0, 2).reshape(D, D))
    g["wkb"] = np.ascontiguousarray(
        inputs["L0_attn_b_Wk"].transpose(1, 0, 2).reshape(D, D))
    g["bqb"] = np.ascontiguousarray(inputs["L0_attn_b_bq"].reshape(NP, 128).T)
    g["bkb"] = np.ascontiguousarray(inputs["L0_attn_b_bk"].reshape(NP, 128).T)
    g["wvdT"] = np.ascontiguousarray(
        inputs["L0_attn_b_Wvd"].transpose(2, 0, 1).reshape(16, 16 * D))
    g["wvu"] = np.ascontiguousarray(
        inputs["L0_attn_b_Wvu"].transpose(1, 0, 2).reshape(16, H * DH))
    g["bvdT"] = np.ascontiguousarray(inputs["L0_attn_b_bvd"].T)
    g["bvu_row"] = np.ascontiguousarray(inputs["L0_attn_b_bvu"].reshape(1, H * DH))
    g["fc1q"] = np.ascontiguousarray(inputs["L0_ffn_fc1_q"])
    g["s1rows"] = np.ascontiguousarray(inputs["L0_ffn_fc1_scales"].reshape(1, 8 * FF))
    g["fc1b"] = np.ascontiguousarray(inputs["L0_ffn_fc1_bias"].reshape(32, 128).T)
    g["fc2q"] = np.ascontiguousarray(inputs["L0_ffn_fc2_q"])
    g["s2rows"] = np.ascontiguousarray(inputs["L0_ffn_fc2_scales"].reshape(1, 32 * D))
    g["fc2b"] = np.ascontiguousarray(inputs["L0_ffn_fc2_bias"].reshape(8, 128).T)
    g["gam"] = np.ascontiguousarray(inputs["L0_ffn_ln_gamma"].reshape(8, 128).T)
    g["bet"] = np.ascontiguousarray(inputs["L0_ffn_ln_beta"].reshape(8, 128).T)
    i_idx = np.arange(128)[:, None]
    q_idx = np.arange(C)[None, :]
    ma = np.zeros((128, 4, 2, C), dtype=bf)
    mb = np.zeros((128, 4, 2, C), dtype=bf)
    for d in range(4):
        keep_a = ((128 * d + i_idx) <= q_idx)
        keep_b = ((128 * d + i_idx) >= q_idx)
        for h2 in range(2):
            ma[:, d, h2, :] = keep_a
            mb[:, d, h2, :] = keep_b
    g["maska"] = np.ascontiguousarray(ma.reshape(128, 4 * 2 * C))
    g["maskb"] = np.ascontiguousarray(mb.reshape(128, 4 * 2 * C))
    g["ones_col"] = np.ones((128, 1), dtype=bf)
    g["ones_row"] = np.ones((1, 128), dtype=bf)
    em = np.zeros((33, 128), dtype=bf)
    em[0, 0:64] = 1.0
    em[32, 64:128] = 1.0
    g["ematD"] = em
    return g


def kernel(**inputs):
    x = np.asarray(inputs["L0_attn_a_in_features"])
    B = x.shape[0]
    g = _prep_inputs(inputs)
    nc = build_program()
    in_maps = []
    for c in range(B):
        m = dict(g)
        m["xT"] = np.ascontiguousarray(x[c].T)
        in_maps.append(m)
    res = run_bass_kernel_spmd(nc, in_maps, core_ids=list(range(B)))
    out = np.empty((B, S, D), dtype=np.float32)
    for c in range(B):
        out[c] = np.asarray(res.results[c]["outT"]).T
    return out
